# version 05
# baseline (speedup 1.0000x reference)
import os
import numpy as np
import ml_dtypes
import concourse.bass as bass
import concourse.mybir as mybir
from concourse.bass_utils import run_bass_kernel_spmd
from contextlib import ExitStack

F32 = mybir.dt.float32
BF16 = mybir.dt.bfloat16
I8 = mybir.dt.int8
I32 = mybir.dt.int32
AF = mybir.ActivationFunctionType
ALU = mybir.AluOpType
AX = mybir.AxisListType

T = 4096
D = 1024
NT = T // 128
DI = 2048
NST = 16
C0 = 512
NC0 = T // C0
TC1 = 256
NTC1 = T // TC1
TCF = 512
NTCF = T // TCF
DT = 0.1
DT2 = 0.01

SEMS = ["pe", "ve", "se", "ge", "sy", "dli", "dl0", "dl1", "st0", "st1", "dst", "dwt", "fence"]
ENG_ATTR = {"pe": "tensor", "ve": "vector", "se": "scalar",
            "ge": "gpsimd", "sy": "sync"}


class Rec:
    __slots__ = ("meth", "a", "kw", "incs", "lp", "fs")

    def __init__(self, meth, a, kw, lp):
        self.meth = meth
        self.a = a
        self.kw = kw
        self.incs = []
        self.lp = lp
        self.fs = 1 << 30

    def then_inc(self, sem, n):
        self.incs.append((sem, n))
        return self


class EngRec:
    def __init__(self, ctx, prog):
        object.__setattr__(self, "_ctx", ctx)
        object.__setattr__(self, "_prog", prog)
        object.__setattr__(self, "_narrow", False)

    _NONCOMPUTE = ("wait_ge", "nop", "engine_nop", "dma_start")

    @staticmethod
    def _free_sz(a):
        for x in a:
            shp = getattr(x, "shape", None)
            if shp is not None and len(shp) >= 2:
                n = 1
                for d in shp[1:]:
                    n *= d
                return n
        return 1 << 30

    def __getattr__(self, meth):
        def call(*a, **kw):
            if self._prog:
                prev = self._prog[-1]
                if meth == "transpose" and prev.meth == "transpose":
                    self._prog.append(Rec("wait_ge",
                                          (self._ctx.sem["fence"], 1), {}, False))
                elif (self._narrow and meth not in self._NONCOMPUTE
                      and prev.meth not in self._NONCOMPUTE
                      and prev.meth != "drain"
                      and (self._free_sz(a) < 128 or prev.fs < 128)):
                    self._prog.append(Rec("drain", (), {}, False))
            r = Rec(meth, a, kw, self._ctx.lp_flag)
            r.fs = self._free_sz(a) if meth not in self._NONCOMPUTE else (1 << 30)
            self._prog.append(r)
            return r
        return call


class Ctx:
    def __init__(self, nc, block, semh):
        self.nc = nc
        self.block = block
        self.sem = semh
        self.cnt = {n: 0 for n in semh}
        self.last_wait = {}
        self.lp_flag = False
        self.prog = {n: [] for n in ENG_ATTR}
        self.e = {}
        for n in ENG_ATTR:
            er = EngRec(self, self.prog[n])
            er._narrow = n in ("ve", "se", "ge")
            self.e[n] = er

    def inc(self, name, instr, n=1):
        instr.then_inc(self.sem[name], n)
        self.cnt[name] += n
        return self.cnt[name]

    def wait(self, engname, semname, val=None):
        if val is None:
            val = self.cnt[semname]
        if val <= 0:
            return
        key = (engname, semname)
        if self.last_wait.get(key, -1) < val:
            self.e[engname].wait_ge(self.sem[semname], val)
            self.last_wait[key] = val

    def barrier(self):
        for en, sn in (("pe", "pe"), ("ve", "ve"), ("se", "se"),
                       ("ge", "ge"), ("sy", "sy")):
            self.e[en].drain()
            self.inc(sn, self.e[en].nop(), 1)
        for en in ("pe", "ve", "se", "ge", "sy"):
            for sn in SEMS:
                self.wait(en, sn)

    def flush(self):
        nc = self.nc
        knop = {}
        for part in os.environ.get("KNOP", "").split(","):
            if ":" in part:
                en, rng_ = part.split(":")
                a_, b_ = rng_.split("-")
                knop.setdefault(en, []).append((int(a_), int(b_)))
        if not hasattr(self, "opidx"):
            self.opidx = {n: 0 for n in ENG_ATTR}
        for name, attr in ENG_ATTR.items():
            prog = list(self.prog[name])
            self.prog[name].clear()
            if not prog:
                continue
            base = self.opidx[name]
            self.opidx[name] += len(prog)

            def section(e, prog=prog, name=name, base=base):
                for idx, r in enumerate(prog):
                    gi = base + idx
                    killwaits = os.environ.get("KNOPW", "")
                    skipm = ("nop", "engine_nop", "dma_start")
                    if name not in killwaits:
                        skipm = skipm + ("wait_ge",)
                    dead = (name in knop
                            and any(a_ <= gi < b_ for a_, b_ in knop[name])
                            and r.meth not in skipm)
                    if dead:
                        inst = e.nop()
                    elif r.lp:
                        with nc.allow_low_precision(reason="bf16 accum"):
                            inst = getattr(e, r.meth)(*r.a, **r.kw)
                    else:
                        inst = getattr(e, r.meth)(*r.a, **r.kw)
                    for s, n in r.incs:
                        inst.then_inc(s, n)
            getattr(self.block, attr)(section)


def build_nc(nphases=6, dbg=False):
    nc = bass.Bass("TRN2", target_bir_lowering=False, detect_race_conditions=False)
    dr = {}

    def din(name, shape, dt=F32):
        dr[name] = nc.dram_tensor(name, shape, dt, kind="ExternalInput").ap()
        return dr[name]

    x_in = din("x", [T, D])
    out_hbm = nc.dram_tensor("out", [T, D], F32, kind="ExternalOutput").ap()
    dbg1 = nc.dram_tensor("dbg1", [128, D], F32, kind="ExternalOutput").ap()
    dbg2 = nc.dram_tensor("dbg2", [128, D], F32, kind="ExternalOutput").ap()
    dbg3 = nc.dram_tensor("dbg3", [128, D], F32, kind="ExternalOutput").ap()
    dbg4 = nc.dram_tensor("dbg4", [128, NT], F32, kind="ExternalOutput").ap()
    dbg5 = nc.dram_tensor("dbg5", [128, 8 * C0], F32, kind="ExternalOutput").ap()
    dbg6 = nc.dram_tensor("dbg6", [128, 32], F32, kind="ExternalOutput").ap()
    dbg7 = nc.dram_tensor("dbg7", [128, 8 * C0], F32, kind="ExternalOutput").ap()
    dbg8 = nc.dram_tensor("dbg8", [128, 8 * T], F32, kind="ExternalOutput").ap()
    dbg9 = nc.dram_tensor("dbg9", [128, T], F32, kind="ExternalOutput").ap()
    dbg10 = nc.dram_tensor("dbg10", [128, 16], F32, kind="ExternalOutput").ap()

    din("bwt", [D, D]); din("omega", [128, 8])
    din("wqt", [128, 8 * D], I8); din("wsrep", [128, D]); din("biasrep", [128, D])
    din("wxzt", [D, 2 * DI]); din("convw", [128, 64]); din("convb", [128, 16])
    din("dtb", [128, 16]); din("dpar", [128, 16]); din("xpt", [DI, 33])
    din("dtw", [1, 2048]); din("alog", [128, 256]); din("opwt", [DI, D])
    din("cw1", [D, D]); din("cb1", [128, 8]); din("cw2", [D, 512]); din("cb2", [128, 4])
    din("cw3", [512, 256]); din("cb3", [128, 2]); din("w4rep", [256, 128]); din("b4", [1, 1])
    din("gw1", [D, 512]); din("gb1", [128, 4]); din("gw2", [512, 256]); din("gb2", [128, 2])
    din("gw3", [128, 2]); din("gb3", [1, 1]); din("arange_", [1, 1])
    din("identbf", [128, 128], BF16); din("identf", [128, 128], F32)
    din("selmat", [33, 33 * 128])

    es0 = ExitStack()

    def sb(name, shape, dt=F32):
        return es0.enter_context(nc.sbuf_tensor(name, shape, dt))

    idb = sb("idb", [128, 128], BF16)
    idf = sb("idf", [128, 128], F32)
    onesrow = sb("onesrow", [1, 128], F32)
    onesrow_b = sb("onesrow_b", [1, 128], BF16)
    c1 = sb("c1", [128, 1], F32)
    cn1 = sb("cn1", [128, 1], F32)
    chalf = sb("chalf", [128, 1], F32)
    ceps = sb("ceps", [128, 1], F32)
    macc = sb("macc", [128, NT], F32)
    sqacc = sb("sqacc", [128, NT], F32)
    mcol = sb("mcol", [128, NT], F32)
    rcol = sb("rcol", [128, NT], F32)
    nmr = sb("nmr", [128, NT], F32)
    scrN = sb("scrN", [128, NT], F32)
    dumpt = sb("dumpt", [128, 1024], BF16)

    bes = ExitStack()
    semh = {n: bes.enter_context(nc.semaphore(n)) for n in SEMS}
    block = bes.enter_context(nc.Block())
    k = Ctx(nc, block, semh)
    pe, ve, se, ge, sy = k.e["pe"], k.e["ve"], k.e["se"], k.e["ge"], k.e["sy"]

    k.inc("dli", sy.dma_start(out=idb[:], in_=dr["identbf"]), 16)
    k.inc("dli", sy.dma_start(out=idf[:], in_=dr["identf"]), 16)
    ve.memset(onesrow[:], 1.0)
    ve.memset(onesrow_b[:], 1.0)
    ve.memset(c1[:], 1.0)
    ve.memset(cn1[:], -1.0)
    ve.memset(chalf[:], 0.5)
    k.inc("fence", ve.memset(ceps[:], 1e-5), 1)
    k.inc("ve", ve.memset(ceps[:], 1e-5), 1)
    k.barrier()
    k.flush()

    def colmath_stats():
        ve.tensor_scalar(mcol[:], macc[:], 1.0 / D, None, ALU.mult)
        ve.tensor_scalar(scrN[:], sqacc[:], 1.0 / D, None, ALU.mult)
        ve.tensor_tensor(rcol[:], mcol[:], mcol[:], ALU.mult)
        k.inc("ve", ve.tensor_tensor(scrN[:], scrN[:], rcol[:], ALU.subtract), 1)
        k.wait("se", "ve")
        k.inc("se", se.activation(scrN[:], scrN[:], AF.Sqrt, bias=ceps[:, 0:1]), 1)
        k.wait("ve", "se")
        ve.reciprocal(rcol[:], scrN[:])
        ve.tensor_tensor(nmr[:], mcol[:], rcol[:], ALU.mult)
        k.inc("ve", ve.tensor_scalar(nmr[:], nmr[:], -1.0, None, ALU.mult), 1)
        k.wait("se", "ve")
        k.wait("sy", "ve")

    def phase_l0ffn(app, src_hbm, first):
        es = ExitStack()

        def p(name, shape, dt=F32):
            return es.enter_context(nc.sbuf_tensor(f"f{app}_{name}", shape, dt))

        with es:
            wq_i8 = p("wq8", [128, 8 * D], I8)
            wq = p("wq", [128, 8 * D], BF16)
            wsr = p("wsr", [128, D], F32)
            bh = p("bh", [128, D], F32)
            xt = [p(f"xt{j}", [128, D], F32) for j in range(2)]
            xln = [p(f"xln{j}", [128, D], F32) for j in range(2)]
            qs = p("qs", [128, D], F32)
            q0b = p("q0b", [128, D], BF16)
            q0i = p("q0i", [128, D], I32)
            uu = p("uu", [128, D], F32)
            cc1 = p("cc1", [128, D], F32)
            qb = p("qb", [128, D], BF16)
            qT = [p(f"qT{j}", [128, D], BF16) for j in range(2)]
            acol = p("acol", [128, 2], F32)
            cqcol = p("cqcol", [128, 2], F32)
            xshc = p("xshc", [128, 2], F32)
            u1 = p("u1", [128, D], F32)
            u2 = p("u2", [128, D], F32)
            res = [p(f"res{j}", [128, D], F32) for j in range(2)]
            psT = es.enter_context(nc.psum_tensor(f"f{app}_psT", [128, 1024], BF16))
            psM = [es.enter_context(nc.psum_tensor(f"f{app}_psM{j}", [128, 512], F32))
                   for j in range(4)]

            k.inc("dli", sy.dma_start(out=wq_i8[:], in_=dr["wqt"]), 16)
            k.inc("dli", sy.dma_start(out=wsr[:], in_=dr["wsrep"]), 16)
            k.inc("dli", sy.dma_start(out=bh[:], in_=dr["biasrep"]), 16)
            k.wait("ve", "dwt")
            k.wait("ve", "dli")
            k.inc("ve", ve.tensor_copy(wq[:], wq_i8[:]), 1)
            k.inc("ve", ve.tensor_scalar(bh[:], bh[:], 0.5, None, ALU.mult), 1)

            trunc = os.environ.get("KTRUNC", "")
            skip_pre = trunc.endswith("np")
            if skip_pre:
                trunc = trunc[:-2]
            if first and not skip_pre:
                sdone = {}
                for i in range(NT):
                    j = i % 2
                    if i >= 2:
                        k.wait("sy", "se", sdone[i - 2])
                    k.inc(f"dl{j}", sy.dma_start(
                        out=xt[j][:], in_=src_hbm[i * 128:(i + 1) * 128, :]), 16)
                    k.wait("se", f"dl{j}")
                    se.activation(dumpt[:], xt[j][:], AF.Identity,
                                  accum_out=macc[:, i:i + 1])
                    sdone[i] = k.inc("se", se.activation(
                        dumpt[:], xt[j][:], AF.Square,
                        accum_out=sqacc[:, i:i + 1]), 1)
                k.barrier()
            colmath_stats()
            if trunc == "prepass":
                k.barrier()
                k.flush()
                return

            res_done = {}
            tpq_done = {}
            evac_done = {}
            qt_copied = {}
            mm_of = {}
            qbd_of = {}
            ntl = NT
            stage = 9
            if trunc.startswith("s"):
                parts = trunc[1:].split("-")
                stage = int(parts[0])
                ntl = int(parts[1]) if len(parts) > 1 else 2
            elif trunc.isdigit():
                ntl = int(trunc)
            for i in range(ntl):
                j = i % 2
                if i >= 2:
                    k.wait("sy", "ve", res_done[i - 2])
                k.inc(f"dl{j}", sy.dma_start(
                    out=xt[j][:], in_=src_hbm[i * 128:(i + 1) * 128, :]), 16)
                k.wait("se", f"dl{j}")
                if i >= 2:
                    k.wait("se", "ve", qbd_of[i - 2])
                lnd = k.inc("se", se.activation(
                    xln[j][:], xt[j][:], AF.Identity,
                    scale=rcol[:, i:i + 1], bias=nmr[:, i:i + 1]), 1)
                k.wait("ve", "se", lnd)
                ve.tensor_reduce(acol[:, j:j + 1], xln[j][:], AX.X, ALU.max,
                                 apply_absolute_value=True)
                ve.tensor_scalar(acol[:, j:j + 1], acol[:, j:j + 1], 1e-6,
                                 0.5 / 127.0, ALU.max, ALU.mult)
                ve.reciprocal(cqcol[:, j:j + 1], acol[:, j:j + 1])
                ve.tensor_scalar(cqcol[:, j:j + 1], cqcol[:, j:j + 1], 0.5, None,
                                 ALU.mult)
                ve.tensor_copy(xshc[:, j:j + 1], acol[:, j:j + 1])
                ve.tensor_scalar_mul(qs[:], xln[j][:], cqcol[:, j:j + 1])
                ve.tensor_copy(q0i[:], qs[:])
                ve.tensor_copy(q0b[:], q0i[:])
                ve.tensor_tensor(uu[:], qs[:], q0b[:], ALU.subtract)
                ve.tensor_scalar(cc1[:], uu[:], 0.5, None, ALU.is_ge)
                ve.tensor_scalar(uu[:], uu[:], -0.5, None, ALU.is_le)
                ve.tensor_tensor(cc1[:], cc1[:], uu[:], ALU.subtract)
                if i >= 1:
                    k.wait("ve", "pe", tpq_done[i - 1])
                qbd_of[i] = qbd = k.inc("ve", ve.tensor_tensor(
                    qb[:], q0b[:], cc1[:], ALU.add), 1)
                if stage <= 1:
                    continue
                k.wait("pe", "ve", qbd)
                if i >= 1:
                    k.wait("pe", "ve", qt_copied[i - 1])
                ist = None
                for c in range(8):
                    ist = pe.transpose(psT[:, c * 128:(c + 1) * 128],
                                       qb[:, c * 128:(c + 1) * 128], idb[:])
                tpq_done[i] = k.inc("pe", ist, 1)
                k.wait("ve", "pe", tpq_done[i])
                if i >= 2:
                    k.wait("ve", "pe", mm_of[i - 2])
                qt_copied[i] = qtc = k.inc("ve", ve.tensor_copy(qT[j][:], psT[:]), 1)
                if stage <= 2:
                    continue
                k.wait("pe", "ve", qtc)
                mb = [psM[2 * j], psM[2 * j + 1]]
                if i >= 2:
                    k.wait("pe", "se", evac_done[i - 2])
                last = None
                for h in range(2):
                    for kc in range(8):
                        last = pe.matmul(mb[h][:],
                                         qT[j][:, kc * 128:(kc + 1) * 128],
                                         wq[:, kc * D + h * 512: kc * D + h * 512 + 512],
                                         start=(kc == 0), stop=(kc == 7))
                mm_of[i] = mmd = k.inc("pe", last, 1)
                if stage <= 3:
                    continue
                if os.environ.get("KEVAC", "") == "v":
                    k.wait("ve", "pe", mmd)
                    ve.tensor_scalar_mul(u1[:, 0:512], mb[0][:], xshc[:, j:j + 1])
                    evac_done[i] = k.inc("ve", ve.tensor_scalar_mul(
                        u1[:, 512:1024], mb[1][:], xshc[:, j:j + 1]), 1)
                    k.wait("se", "ve", evac_done[i])
                    k.inc("se", se.engine_nop() if False else se.activation(
                        dumpt[:, 0:1], xshc[:, 0:1], AF.Identity), 1)
                    evac_done[i] = k.cnt["se"]
                else:
                    k.wait("se", "pe", mmd)
                    se.activation(u1[:, 0:512], mb[0][:], AF.Identity,
                                  scale=xshc[:, j:j + 1])
                    evac_done[i] = k.inc("se", se.activation(
                        u1[:, 512:1024], mb[1][:], AF.Identity,
                        scale=xshc[:, j:j + 1]), 1)
                if stage <= 4:
                    continue
                k.wait("ve", "se", evac_done[i])
                ve.tensor_tensor(u2[:], u1[:], wsr[:], ALU.mult)
                u2d = ve.tensor_tensor(u2[:], u2[:], bh[:], ALU.add)
                if stage <= 5:
                    k.inc("ve", u2d, 1)
                    continue
                if i >= 2:
                    k.wait("ve", f"st{j}")
                res_done[i] = k.inc("ve", ve.scalar_tensor_tensor(
                    res[j][:], u2[:], c1[:, 0:1], xt[j][:], ALU.mult, ALU.add,
                    accum_out=macc[:, i:i + 1]), 1)
                if stage <= 6:
                    continue
                k.wait("se", "ve", res_done[i])
                k.inc("se", se.activation(dumpt[:], res[j][:], AF.Square,
                                          accum_out=sqacc[:, i:i + 1]), 1)
                if stage <= 7:
                    continue
                k.wait("sy", "ve", res_done[i])
                k.inc(f"st{j}", sy.dma_start(
                    out=out_hbm[i * 128:(i + 1) * 128, :], in_=res[j][:]), 16)
                if dbg and app == 0 and i == 0:
                    k.wait("sy", "se")
                    k.inc("dst", sy.dma_start(out=dbg1, in_=xln[0][:]), 16)
                    k.inc("dst", sy.dma_start(out=dbg2, in_=u1[:]), 16)
                    k.inc("dst", sy.dma_start(out=dbg3, in_=res[j][:]), 16)
            if dbg and app == 0:
                k.inc("dst", sy.dma_start(out=dbg4[0:128, 0:NT], in_=rcol[:]), 16)
            k.barrier()
            k.flush()

    def phase_l0attn():
        es = ExitStack()

        def p(name, shape, dt=F32):
            return es.enter_context(nc.sbuf_tensor(f"a0_{name}", shape, dt))

        with es:
            bwt = p("bwt", [128, 8 * D], BF16)
            omg = p("omg", [128, 8], F32)
            sfp = p("sfp", [128, 8], F32)
            uom = p("uom", [128, 8], F32)
            ctc = p("ctc", [128, 8], F32)
            s2c = p("s2c", [128, 8], F32)
            t8a = p("t8a", [128, 8], F32)
            t8b = p("t8b", [128, 8], F32)
            ctab = p("ctab", [128, 8 * (C0 + 2)], F32)
            ptab = p("ptab", [128, 8 * (C0 + 2)], F32)
            su = p("su", [128, 8], F32)
            sv = p("sv", [128, 8], F32)
            av = p("av", [128, 8], F32)
            au = p("au", [128, 8], F32)
            nsv = p("nsv", [128, 8], F32)
            pqe = p("pqe", [128, 16], F32)
            xt = [p(f"xt{j}", [128, D], F32) for j in range(8)]
            xlb = p("xlb", [128, D], BF16)
            xT = p("xT", [128, 8 * C0], BF16)
            bx = p("bx", [128, 8 * C0], F32)
            w1 = p("w1", [128, C0], F32)
            w2 = p("w2", [128, C0], F32)
            ppv = p("ppv", [128, C0], F32)
            qqv = p("qqv", [128, C0], F32)
            onesC = p("onesC", [128, C0], F32)
            t4 = p("t4", [128, C0], F32)
            yv = p("yv", [128, 8 * C0], F32)
            res = [p(f"res{j}", [128, D], F32) for j in range(2)]
            psT = es.enter_context(nc.psum_tensor("a0_psT", [128, 1024], BF16))
            psB = [es.enter_context(nc.psum_tensor(f"a0_psB{j}", [128, C0], F32))
                   for j in range(2)]
            psY = [es.enter_context(nc.psum_tensor(f"a0_psY{j}", [128, 1024], F32))
                   for j in range(2)]

            ve.memset(onesC[:], 1.0)
            colmath_stats()
            k.inc("dwt", ge.dma_start(
                out=bwt[:].rearrange("p (c o) -> p c o", c=8),
                in_=dr["bwt"].rearrange("(c p) o -> p c o", p=128)), 16)
            k.inc("dli", sy.dma_start(out=omg[:], in_=dr["omega"]), 16)
            k.wait("se", "dli")
            se.activation(sfp[:], omg[:], AF.Abs)
            se.activation(sfp[:], sfp[:], AF.Exp, scale=cn1[:, 0:1])
            spd = k.inc("se", se.activation(sfp[:], sfp[:], AF.Ln, bias=c1[:, 0:1]), 1)
            k.wait("ve", "se", spd)
            k.wait("ve", "dli")
            ve.tensor_scalar(omg[:], omg[:], 0.0, None, ALU.max)
            ve.tensor_tensor(omg[:], omg[:], sfp[:], ALU.add)
            ve.tensor_scalar(uom[:], omg[:], 0.005, None, ALU.mult)
            ve.tensor_scalar(ctc[:], uom[:], -1.0, 1.0, ALU.mult, ALU.add)
            ve.tensor_scalar(s2c[:], uom[:], -1.0, 2.0, ALU.mult, ALU.add)
            ve.tensor_tensor(s2c[:], s2c[:], uom[:], ALU.mult)
            M = C0 + 2
            for fc in range(8):
                base = fc * M

                def cv(a, b):
                    return ctab[:, base + a: base + b]

                def pv(a, b):
                    return ptab[:, base + a: base + b]

                ve.memset(cv(0, 1), 1.0)
                ve.memset(pv(0, 1), 0.0)
                ve.tensor_copy(cv(1, 2), ctc[:, fc:fc + 1])
                ve.memset(pv(1, 2), 1.0)
                n = 2
                while n < M:
                    kk = min(n - 1, M - n)
                    cm = ctab[:, base + n - 1: base + n]
                    pm = ptab[:, base + n - 1: base + n]
                    ve.tensor_tensor(t8a[:, 0:1], s2c[:, fc:fc + 1], pm, ALU.mult)
                    ve.tensor_scalar_mul(cv(n, n + kk), pv(1, 1 + kk), t8a[:, 0:1])
                    ve.scalar_tensor_tensor(cv(n, n + kk), cv(1, 1 + kk), cm[:, 0:1],
                                            cv(n, n + kk), ALU.mult, ALU.subtract)
                    ve.tensor_scalar_mul(pv(n, n + kk), cv(1, 1 + kk), pm[:, 0:1])
                    ve.scalar_tensor_tensor(pv(n, n + kk), pv(1, 1 + kk), cm[:, 0:1],
                                            pv(n, n + kk), ALU.mult, ALU.add)
                    n += kk
            ve.memset(su[:], 0.0)
            ve.memset(sv[:], 0.0)
            ve.memset(nsv[:], 0.0)
            ve.memset(av[:], 0.0)
            k.inc("ve", ve.memset(au[:], 0.0), 1)
            k.barrier()

            for tc in range(NC0):
                for s in range(4):
                    ti = tc * 4 + s
                    buf = xt[ti % 8]
                    k.inc(f"dl{ti % 2}", sy.dma_start(
                        out=buf[:], in_=out_hbm[ti * 128:(ti + 1) * 128, :]), 16)
                    k.wait("se", f"dl{ti % 2}")
                    k.wait("se", "pe")
                    lnd = k.inc("se", se.activation(
                        xlb[:], buf[:], AF.Identity,
                        scale=rcol[:, ti:ti + 1], bias=nmr[:, ti:ti + 1]), 1)
                    k.wait("pe", "se", lnd)
                    k.wait("pe", "ve")
                    ist = None
                    for c in range(8):
                        ist = pe.transpose(psT[:, c * 128:(c + 1) * 128],
                                           xlb[:, c * 128:(c + 1) * 128], idb[:])
                    tpd = k.inc("pe", ist, 1)
                    k.wait("ve", "pe", tpd)
                    lastc = None
                    for c in range(8):
                        lastc = ve.tensor_copy(
                            xT[:, c * C0 + s * 128: c * C0 + s * 128 + 128],
                            psT[:, c * 128:(c + 1) * 128])
                    k.inc("ve", lastc, 1)
                k.wait("pe", "ve")
                bx_ev = {}
                for ec in range(8):
                    pb = psB[ec % 2]
                    if ec >= 2:
                        k.wait("pe", "se", bx_ev[ec - 2])
                    last = None
                    for dc in range(8):
                        last = pe.matmul(
                            pb[:], bwt[:, dc * D + ec * 128: dc * D + ec * 128 + 128],
                            xT[:, dc * C0:(dc + 1) * C0],
                            start=(dc == 0), stop=(dc == 7))
                    mmd = k.inc("pe", last, 1)
                    k.wait("se", "pe", mmd)
                    bx_ev[ec] = k.inc("se", se.activation(
                        bx[:, ec * C0:(ec + 1) * C0], pb[:], AF.Identity), 1)
                k.wait("ve", "se")
                for fc in range(8):
                    tb = fc * M
                    bxs = bx[:, fc * C0:(fc + 1) * C0]
                    ve.scalar_tensor_tensor(w1[:], bxs, chalf[0:128, 0:1], ctab[:, tb:tb + C0],
                                            ALU.bypass, ALU.mult)
                    ve.tensor_scalar(w1[:], w1[:], DT2, None, ALU.mult)
                    ve.scalar_tensor_tensor(w2[:], bxs, chalf[0:128, 0:1], ptab[:, tb:tb + C0],
                                            ALU.bypass, ALU.mult)
                    ve.tensor_scalar(w2[:], w2[:], DT2, None, ALU.mult)
                    ve.tensor_tensor_scan(ppv[:], onesC[:], w1[:], 0.0, ALU.mult, ALU.add)
                    ve.tensor_tensor_scan(qqv[:], onesC[:], w2[:], 0.0, ALU.mult, ALU.add)
                    ys = yv[:, fc * C0:(fc + 1) * C0]
                    ve.scalar_tensor_tensor(ys, ppv[:], av[:, fc:fc + 1],
                                            ptab[:, tb + 1:tb + 1 + C0],
                                            ALU.add, ALU.mult)
                    ve.scalar_tensor_tensor(ys, ptab[:, tb:tb + C0], nsv[:, fc:fc + 1],
                                            ys, ALU.mult, ALU.add)
                    ve.tensor_tensor(t4[:], qqv[:], ctab[:, tb + 1:tb + 1 + C0], ALU.mult)
                    ve.tensor_tensor(ys, ys, t4[:], ALU.subtract)
                    ve.tensor_copy(pqe[:, fc:fc + 1], ppv[:, C0 - 1:C0])
                    ve.tensor_copy(pqe[:, 8 + fc:9 + fc], qqv[:, C0 - 1:C0])
                EE = C0 - 1
                ct3 = ctab[:].rearrange("p (c m) -> p c m", m=M)
                pt3 = ptab[:].rearrange("p (c m) -> p c m", m=M)
                pE = pt3[:, :, EE]
                pE1 = pt3[:, :, EE + 1]
                cE = ct3[:, :, EE]
                cE1 = ct3[:, :, EE + 1]
                PEc = pqe[:, 0:8]
                QEc = pqe[:, 8:16]
                ve.tensor_tensor(t8a[:], pE1, av[:], ALU.mult)
                ve.tensor_tensor(t8b[:], pE, sv[:], ALU.mult)
                ve.tensor_tensor(t8a[:], t8a[:], t8b[:], ALU.subtract)
                ve.tensor_tensor(t8b[:], pE1, PEc, ALU.mult)
                ve.tensor_tensor(t8a[:], t8a[:], t8b[:], ALU.add)
                ve.tensor_tensor(t8b[:], cE1, QEc, ALU.mult)
                ve.tensor_tensor(t8a[:], t8a[:], t8b[:], ALU.subtract)
                ve.tensor_tensor(uom[:], pE1, au[:], ALU.mult)
                ve.tensor_tensor(t8b[:], pE, su[:], ALU.mult)
                ve.tensor_tensor(uom[:], uom[:], t8b[:], ALU.subtract)
                ve.tensor_tensor(t8b[:], pE1, PEc, ALU.mult)
                ve.tensor_tensor(sfp[:], pE, PEc, ALU.mult)
                ve.tensor_tensor(t8b[:], t8b[:], sfp[:], ALU.subtract)
                ve.scalar_tensor_tensor(uom[:], t8b[:], 10.0, uom[:], ALU.mult, ALU.add)
                ve.tensor_tensor(t8b[:], cE1, QEc, ALU.mult)
                ve.tensor_tensor(sfp[:], cE, QEc, ALU.mult)
                ve.tensor_tensor(t8b[:], t8b[:], sfp[:], ALU.subtract)
                ve.scalar_tensor_tensor(uom[:], t8b[:], -10.0, uom[:], ALU.mult, ALU.add)
                ve.tensor_copy(sv[:], t8a[:])
                ve.tensor_copy(su[:], uom[:])
                ve.tensor_scalar(nsv[:], sv[:], -1.0, None, ALU.mult)
                ve.tensor_scalar(t8a[:], ctc[:], 2.0, -1.0, ALU.mult, ALU.add)
                ve.tensor_tensor(av[:], t8a[:], sv[:], ALU.mult)
                ve.scalar_tensor_tensor(av[:], su[:], DT, av[:], ALU.mult, ALU.add)
                ve.tensor_scalar(t8a[:], ctc[:], -20.0, 20.0, ALU.mult, ALU.add)
                ve.tensor_tensor(au[:], t8a[:], sv[:], ALU.mult)
                k.inc("ve", ve.tensor_tensor(au[:], su[:], au[:], ALU.subtract), 1)
                if dbg and tc == 0:
                    k.wait("sy", "ve")
                    k.wait("sy", "se")
                    k.inc("dst", sy.dma_start(out=dbg8[0:128, 0:8 * (C0 + 2)],
                                              in_=ptab[:]), 16)
                    k.inc("dst", sy.dma_start(out=dbg9[0:128, 0:8 * (C0 + 2) - 128],
                                              in_=ctab[:, 0:8 * (C0 + 2) - 128]), 16)
                    k.inc("dst", sy.dma_start(out=dbg5, in_=yv[:]), 16)
                    k.inc("dst", sy.dma_start(out=dbg7, in_=bx[:]), 16)
                    k.inc("dst", sy.dma_start(out=dbg6[0:128, 0:8], in_=su[:]), 16)
                    k.inc("dst", sy.dma_start(out=dbg6[0:128, 8:16], in_=sv[:]), 16)
                    k.inc("dst", sy.dma_start(out=dbg6[0:128, 16:24], in_=av[:]), 16)
                    k.inc("dst", sy.dma_start(out=dbg6[0:128, 24:32], in_=au[:]), 16)
                    k.wait("sy", "dst")
                for s in range(4):
                    ti = tc * 4 + s
                    py = psY[s % 2]
                    k.wait("pe", "ve")
                    last = None
                    for c in range(8):
                        last = pe.transpose(py[:, c * 128:(c + 1) * 128],
                                            yv[:, c * C0 + s * 128: c * C0 + s * 128 + 128],
                                            idf[:])
                    tpd = k.inc("pe", last, 1)
                    k.wait("ve", "pe", tpd)
                    k.wait("ve", f"st{s % 2}")
                    rd = k.inc("ve", ve.scalar_tensor_tensor(
                        res[s % 2][:], py[:], c1[:, 0:1], xt[ti % 8][:],
                        ALU.mult, ALU.add, accum_out=macc[:, ti:ti + 1]), 1)
                    k.wait("se", "ve", rd)
                    k.inc("se", se.activation(dumpt[:], res[s % 2][:], AF.Square,
                                              accum_out=sqacc[:, ti:ti + 1]), 1)
                    k.wait("sy", "ve", rd)
                    k.inc(f"st{s % 2}", sy.dma_start(
                        out=out_hbm[ti * 128:(ti + 1) * 128, :], in_=res[s % 2][:]), 16)
                k.barrier()
            k.flush()

    def phase_l1ffn(app):
        es = ExitStack()

        def p(name, shape, dt=F32):
            return es.enter_context(nc.sbuf_tensor(f"c{app}_{name}", shape, dt))

        with es:
            w1s = p("w1s", [128, 8 * D], BF16)
            w2s = p("w2s", [128, 8 * 512], BF16)
            w3s = p("w3s", [128, 4 * 256], BF16)
            w4s = p("w4s", [128, 2 * 128], BF16)
            outb = p("outb", [128, 8 * T], F32)
            tfraw = p("tfraw", [128, T], BF16)
            obf = p("obf", [128, 8 * TCF], BF16)
            h1b = p("h1b", [128, 8 * TCF], BF16)
            h2b = p("h2b", [128, 4 * TCF], BF16)
            h3b = p("h3b", [128, 2 * TCF], BF16)
            cbb = p("cbb", [128, TCF], F32)
            scr4k = p("scr4k", [128, 1024], F32)
            pta = scr4k[:, 0:TCF]
            ptb = scr4k[:, TCF:2 * TCF]
            cb1s = p("cb1s", [128, 8], F32)
            cb2s = p("cb2s", [128, 4], F32)
            cb3s = p("cb3s", [128, 2], F32)
            abcol = p("abcol", [128, 2], F32)
            b4col = p("b4col", [128, 1], F32)
            gfcol = p("gfcol", [128, 8], F32)
            gfb = p("gfb", [128, 8], BF16)
            xt = [p(f"xt{j}", [128, D], F32) for j in range(2)]
            xlnf = scr4k
            res = [scr4k, p("res1", [128, D], F32)]
            psT = es.enter_context(nc.psum_tensor(f"c{app}_psT", [128, 1024], F32))
            psF = [es.enter_context(nc.psum_tensor(f"c{app}_psF{j}", [128, 512], F32))
                   for j in range(2)]
            psG = es.enter_context(nc.psum_tensor(f"c{app}_psG", [128, 8], F32))
            psC = es.enter_context(nc.psum_tensor(f"c{app}_psC", [128, 1024], F32))

            k.inc("dwt", ge.dma_start(
                out=w1s[:].rearrange("p (c o) -> p c o", c=8),
                in_=dr["cw1"].rearrange("(c p) o -> p c o", p=128)), 16)
            k.inc("dwt", ge.dma_start(
                out=w2s[:].rearrange("p (c o) -> p c o", c=8),
                in_=dr["cw2"].rearrange("(c p) o -> p c o", p=128)), 16)
            k.inc("dwt", ge.dma_start(
                out=w3s[:].rearrange("p (c o) -> p c o", c=4),
                in_=dr["cw3"].rearrange("(c p) o -> p c o", p=128)), 16)
            k.inc("dwt", ge.dma_start(
                out=w4s[:].rearrange("p (c o) -> p c o", c=2),
                in_=dr["w4rep"].rearrange("(c p) o -> p c o", p=128)), 16)
            k.inc("dli", sy.dma_start(out=cb1s[:], in_=dr["cb1"]), 16)
            k.inc("dli", sy.dma_start(out=cb2s[:], in_=dr["cb2"]), 16)
            k.inc("dli", sy.dma_start(out=cb3s[:], in_=dr["cb3"]), 16)

            colmath_stats()
            lnds = {}
            for i in range(NT):
                j = i % 2
                if i >= 2:
                    k.wait("sy", "se", lnds[i - 2])
                k.inc(f"dl{j}", sy.dma_start(
                    out=xt[j][:], in_=out_hbm[i * 128:(i + 1) * 128, :]), 16)
                k.wait("se", f"dl{j}")
                k.wait("se", "pe")
                lnd = k.inc("se", se.activation(
                    xlnf[:], xt[j][:], AF.Identity,
                    scale=rcol[:, i:i + 1], bias=nmr[:, i:i + 1]), 1)
                lnds[i] = lnd
                k.wait("pe", "se", lnd)
                k.wait("pe", "se")
                ist = None
                for c in range(8):
                    ist = pe.transpose(psT[:, c * 128:(c + 1) * 128],
                                       xlnf[:, c * 128:(c + 1) * 128], idf[:])
                tpd = k.inc("pe", ist, 1)
                k.wait("se", "pe", tpd)
                lastc = None
                for c in range(8):
                    lastc = se.activation(
                        outb[:, c * T + i * 128: c * T + i * 128 + 128],
                        psT[:, c * 128:(c + 1) * 128], AF.Identity)
                k.inc("se", lastc, 1)
            k.barrier()
            for c in range(8):
                ve.tensor_reduce(gfcol[:, c:c + 1], outb[:, c * T:(c + 1) * T],
                                 AX.X, ALU.add)
            ve.tensor_scalar(gfcol[:], gfcol[:], 1.0 / T, None, ALU.mult)
            k.inc("ve", ve.tensor_copy(gfb[:], gfcol[:]), 1)
            k.wait("pe", "ve")
            gw1 = h1b
            gw2 = h2b[:, 0:4 * 256]
            gw3 = h3b[:, 0:2]
            gb1 = p("gb1s", [128, 4], F32)
            gb2 = p("gb2s", [128, 2], F32)
            gb3 = p("gb3s", [1, 1], F32)
            b4r = p("b4r", [1, 1], F32)
            arr = p("arr", [1, 1], F32)
            g1b = p("g1b", [128, 4], BF16)
            g2b = p("g2b", [128, 2], BF16)
            gfv = p("gfv", [1, 1], F32)
            abrow = p("abrow", [1, 2], F32)
            k.inc("dwt", ge.dma_start(
                out=gw1.rearrange("p (c o) -> p c o", c=8),
                in_=dr["gw1"].rearrange("(c p) o -> p c o", p=128)), 16)
            k.inc("dwt", ge.dma_start(
                out=gw2.rearrange("p (c o) -> p c o", c=4),
                in_=dr["gw2"].rearrange("(c p) o -> p c o", p=128)), 16)
            k.inc("dwt", ge.dma_start(out=gw3, in_=dr["gw3"]), 16)
            k.inc("dli", sy.dma_start(out=gb1[:], in_=dr["gb1"]), 16)
            k.inc("dli", sy.dma_start(out=gb2[:], in_=dr["gb2"]), 16)
            k.inc("dli", sy.dma_start(out=gb3[:], in_=dr["gb3"]), 16)
            k.inc("dli", sy.dma_start(out=b4r[:], in_=dr["b4"]), 16)
            k.inc("dli", sy.dma_start(out=arr[:], in_=dr["arange_"]), 16)
            k.wait("pe", "dwt")
            k.wait("pe", "dli")
            for mc2 in range(4):
                last = None
                for kc in range(8):
                    last = pe.matmul(psG[:, mc2:mc2 + 1],
                                     gw1[:, kc * 512 + mc2 * 128: kc * 512 + mc2 * 128 + 128],
                                     gfb[:, kc:kc + 1],
                                     start=(kc == 0), stop=(kc == 7))
                k.inc("pe", last, 1)
            k.wait("se", "pe")
            lastg = None
            for mc2 in range(4):
                lastg = se.activation(g1b[:, mc2:mc2 + 1], psG[:, mc2:mc2 + 1],
                                      AF.Gelu, bias=gb1[:, mc2:mc2 + 1])
            k.inc("se", lastg, 1)
            k.wait("pe", "se")
            for mc2 in range(2):
                last = None
                for kc in range(4):
                    last = pe.matmul(psG[:, 4 + mc2:5 + mc2],
                                     gw2[:, kc * 256 + mc2 * 128: kc * 256 + mc2 * 128 + 128],
                                     g1b[:, kc:kc + 1],
                                     start=(kc == 0), stop=(kc == 3))
                k.inc("pe", last, 1)
            k.wait("se", "pe")
            lastg = None
            for mc2 in range(2):
                lastg = se.activation(g2b[:, mc2:mc2 + 1], psG[:, 4 + mc2:5 + mc2],
                                      AF.Gelu, bias=gb2[:, mc2:mc2 + 1])
            k.inc("se", lastg, 1)
            k.wait("pe", "se")
            last = None
            for kc in range(2):
                last = pe.matmul(psG[0:1, 6:7], gw3[:, kc:kc + 1], g2b[:, kc:kc + 1],
                                 start=(kc == 0), stop=(kc == 1))
            k.inc("pe", last, 1)
            k.wait("se", "pe")
            k.inc("se", se.activation(gfv[:], psG[0:1, 6:7], AF.Sigmoid,
                                      bias=gb3[0:1, 0:1]), 1)
            k.wait("ve", "se")
            ve.tensor_scalar(abrow[:, 0:1], gfv[:], 0.7, -0.5, ALU.mult, ALU.add)
            ve.tensor_tensor(abrow[:, 0:1], abrow[:, 0:1], arr[:], ALU.mult)
            ve.tensor_scalar(abrow[:, 0:1], abrow[:, 0:1], 2.0, 0.1, ALU.mult, ALU.add)
            k.inc("ve", ve.tensor_scalar(abrow[:, 1:2], arr[:], 0.6, None, ALU.mult), 1)
            k.wait("pe", "ve")
            k.inc("pe", pe.matmul(psG[:, 0:2], onesrow[:], abrow[:],
                                  start=True, stop=True), 1)
            k.inc("pe", pe.matmul(psG[:, 2:3], onesrow[:], b4r[:],
                                  start=True, stop=True), 1)
            k.wait("ve", "pe")
            ve.tensor_copy(abcol[:], psG[:, 0:2])
            k.inc("ve", ve.tensor_copy(b4col[:], psG[:, 2:3]), 1)
            if dbg and app == 0:
                k.wait("sy", "ve")
                k.inc("dst", sy.dma_start(out=dbg10[0:128, 0:2], in_=abcol[:]), 16)
                k.inc("dst", sy.dma_start(out=dbg10[0:128, 2:3], in_=b4col[:], allow_slow_non_contiguous=True), 16)
                k.inc("dst", sy.dma_start(out=dbg10[0:128, 3:11], in_=gfcol[:]), 16)
                k.wait("sy", "dst")
            k.barrier()

            sign = [1.0, 1.0, -1.0, -1.0, -1.0, -1.0, 1.0, 1.0]
            src = [4, 5, 6, 7, 0, 1, 2, 3]
            for cyc in range(3):
                for tc in range(NTCF):
                    t0 = tc * TCF
                    k.wait("ve", "se")
                    k.wait("ve", "pe")
                    lastc = None
                    for c in range(8):
                        lastc = ve.tensor_copy(obf[:, c * TCF:(c + 1) * TCF],
                                               outb[:, c * T + t0: c * T + t0 + TCF])
                    obd = k.inc("ve", lastc, 1)
                    k.wait("pe", "ve", obd)
                    for oc in range(8):
                        pf = psF[oc % 2]
                        k.wait("pe", "se")
                        last = None
                        for kc in range(8):
                            last = pe.matmul(
                                pf[:, 0:TCF],
                                w1s[:, kc * D + oc * 128: kc * D + oc * 128 + 128],
                                obf[:, kc * TCF:(kc + 1) * TCF],
                                start=(kc == 0), stop=(kc == 7))
                        mmd = k.inc("pe", last, 1)
                        k.wait("se", "pe", mmd)
                        k.inc("se", se.activation(
                            h1b[:, oc * TCF:(oc + 1) * TCF], pf[:, 0:TCF],
                            AF.Gelu, bias=cb1s[:, oc:oc + 1]), 1)
                    for oc in range(4):
                        pf = psF[oc % 2]
                        k.wait("pe", "se")
                        last = None
                        for kc in range(8):
                            last = pe.matmul(
                                pf[:, 0:TCF],
                                w2s[:, kc * 512 + oc * 128: kc * 512 + oc * 128 + 128],
                                h1b[:, kc * TCF:(kc + 1) * TCF],
                                start=(kc == 0), stop=(kc == 7))
                        mmd = k.inc("pe", last, 1)
                        k.wait("se", "pe", mmd)
                        k.inc("se", se.activation(
                            h2b[:, oc * TCF:(oc + 1) * TCF], pf[:, 0:TCF],
                            AF.Gelu, bias=cb2s[:, oc:oc + 1]), 1)
                    for oc in range(2):
                        pf = psF[oc % 2]
                        k.wait("pe", "se")
                        last = None
                        for kc in range(4):
                            last = pe.matmul(
                                pf[:, 0:TCF],
                                w3s[:, kc * 256 + oc * 128: kc * 256 + oc * 128 + 128],
                                h2b[:, kc * TCF:(kc + 1) * TCF],
                                start=(kc == 0), stop=(kc == 3))
                        mmd = k.inc("pe", last, 1)
                        k.wait("se", "pe", mmd)
                        k.inc("se", se.activation(
                            h3b[:, oc * TCF:(oc + 1) * TCF], pf[:, 0:TCF],
                            AF.Gelu, bias=cb3s[:, oc:oc + 1]), 1)
                    pf = psF[0]
                    k.wait("pe", "se")
                    last = None
                    for kc in range(2):
                        last = pe.matmul(pf[:, 0:TCF],
                                         w4s[:, kc * 128:(kc + 1) * 128],
                                         h3b[:, kc * TCF:(kc + 1) * TCF],
                                         start=(kc == 0), stop=(kc == 1))
                    mmd = k.inc("pe", last, 1)
                    k.wait("se", "pe", mmd)
                    k.inc("se", se.activation(tfraw[:, t0:t0 + TCF], pf[:, 0:TCF],
                                              AF.Identity), 1)
                k.barrier()
                for tc in range(NTCF):
                    t0 = tc * TCF
                    k.wait("se", "ve")
                    ssd = k.inc("se", se.activation(cbb[:], tfraw[:, t0:t0 + TCF],
                                                    AF.Sigmoid, bias=b4col[:, 0:1]), 1)
                    k.wait("ve", "se", ssd)
                    ve.tensor_scalar(cbb[:], cbb[:], abcol[:, 1:2], abcol[:, 0:1],
                                     ALU.mult, ALU.add)
                    lastp = None
                    for a in (0, 1, 2, 3):
                        b = src[a]
                        oa = outb[:, a * T + t0: a * T + t0 + TCF]
                        ob = outb[:, b * T + t0: b * T + t0 + TCF]
                        ve.tensor_tensor(pta, cbb[:], ob, ALU.mult)
                        ve.tensor_tensor(ptb, cbb[:], oa, ALU.mult)
                        sa = c1 if sign[a] > 0 else cn1
                        sb_ = c1 if sign[b] > 0 else cn1
                        ve.scalar_tensor_tensor(oa, pta, sa[:, 0:1], oa,
                                                ALU.mult, ALU.add)
                        lastp = ve.scalar_tensor_tensor(ob, ptb, sb_[:, 0:1], ob,
                                                        ALU.mult, ALU.add)
                    k.inc("ve", lastp, 1)
                if dbg and app == 0 and cyc == 0:
                    k.wait("sy", "ve")
                    k.wait("sy", "se")
                    k.inc("dst", sy.dma_start(out=dbg8, in_=outb[:]), 16)
                    k.inc("dwt", ge.dma_start(out=dbg9, in_=tfraw[:]), 16)
                    k.wait("sy", "dst")
                    k.wait("sy", "dwt")
                k.barrier()

            rds = {}
            for i in range(NT):
                j = i % 2
                if i >= 2:
                    k.wait("sy", "ve", rds[i - 2])
                k.inc(f"dl{j}", sy.dma_start(
                    out=xt[j][:], in_=out_hbm[i * 128:(i + 1) * 128, :]), 16)
                k.wait("pe", "ve")
                last = None
                for c in range(8):
                    last = pe.transpose(psC[:, c * 128:(c + 1) * 128],
                                        outb[:, c * T + i * 128: c * T + i * 128 + 128],
                                        idf[:])
                tpd = k.inc("pe", last, 1)
                k.wait("ve", "pe", tpd)
                k.wait("ve", f"dl{j}")
                if i >= 2:
                    k.wait("ve", f"st{j}")
                rd = k.inc("ve", ve.scalar_tensor_tensor(
                    res[j][:], psC[:], chalf[:, 0:1], xt[j][:], ALU.mult, ALU.add,
                    accum_out=macc[:, i:i + 1]), 1)
                rds[i] = rd
                k.wait("se", "ve", rd)
                k.inc("se", se.activation(dumpt[:], res[j][:], AF.Square,
                                          accum_out=sqacc[:, i:i + 1]), 1)
                k.wait("sy", "ve", rd)
                k.inc(f"st{j}", sy.dma_start(
                    out=out_hbm[i * 128:(i + 1) * 128, :], in_=res[j][:]), 16)
            k.barrier()
            k.flush()

    def phase_l1attn():
        es = ExitStack()

        def p(name, shape, dt=F32):
            return es.enter_context(nc.sbuf_tensor(f"m_{name}", shape, dt))

        with es:
            wxz = p("wxz", [128, 8 * 2 * DI], BF16)
            opw = p("opw", [128, 16 * D], BF16)
            xpt = p("xpt", [128, 16 * 33], BF16)
            dtws = p("dtws", [1, 2048], BF16)
            alg = p("alg", [128, 256], F32)
            acoef = alg
            cwv = p("cwv", [128, 64], F32)
            cbv = p("cbv", [128, 16], F32)
            dtbv = p("dtbv", [128, 16], F32)
            dpv = p("dpv", [128, 16], F32)
            xt = [p(f"xt{j}", [128, D], F32) for j in range(2)]
            xlb = p("xlb", [128, D], BF16)
            xTb = p("xTb", [128, 8 * TC1], BF16)
            xin = p("xin", [128, 16 * (TC1 + 3)], BF16)
            ubf = p("ubf", [128, 16 * TC1], BF16)
            xcs = p("xcs", [128, TC1], F32)
            bcb = p("bcb", [33, TC1], BF16)
            sel = p("sel", [33, 33 * 128], BF16)
            dtrb = p("dtrb", [1, TC1], BF16)
            brep = p("brep", [128, 16 * TC1], BF16)
            cint = p("cint", [128, 16 * TC1], BF16)
            dsc = p("dsc", [128, TC1], F32)
            rsc = p("rsc", [128, TC1], F32)
            dub = p("dub", [128, TC1], BF16)
            asc = [p(f"asc{j}", [128, TC1], F32) for j in range(2)]
            bsc = [p(f"bsc{j}", [128, TC1], BF16) for j in range(2)]
            hint = p("hint", [128, 16 * TC1], BF16)
            tmpb = p("tmpb", [128, 16 * TC1], BF16)
            hstate = p("hstate", [128, 256], BF16)
            ybuf = p("ybuf", [128, 16 * TC1], BF16)
            zsb = p("zsb", [128, 16 * TC1], BF16)
            mbf = ybuf
            res0 = p("res0", [128, D], F32)
            res = [res0, res0]
            psT = es.enter_context(nc.psum_tensor("m_psT", [128, 1024], BF16))
            psA = [es.enter_context(nc.psum_tensor(f"m_psA{j}", [128, 512], F32))
                   for j in range(2)]
            ps33 = es.enter_context(nc.psum_tensor("m_ps33", [33, TC1], F32))
            psO = es.enter_context(nc.psum_tensor("m_psO", [128, 1024], F32))

            k.inc("dwt", ge.dma_start(
                out=wxz[:].rearrange("p (c o) -> p c o", c=8),
                in_=dr["wxzt"].rearrange("(c p) o -> p c o", p=128)), 16)
            k.inc("dwt", ge.dma_start(
                out=opw[:].rearrange("p (c o) -> p c o", c=16),
                in_=dr["opwt"].rearrange("(c p) o -> p c o", p=128)), 16)
            k.inc("dwt", ge.dma_start(
                out=xpt[:].rearrange("p (c o) -> p c o", c=16),
                in_=dr["xpt"].rearrange("(c p) o -> p c o", p=128)), 16)
            k.inc("dwt", ge.dma_start(out=dtws[:], in_=dr["dtw"]), 16)
            k.inc("dli", sy.dma_start(out=alg[:], in_=dr["alog"]), 16)
            k.inc("dli", sy.dma_start(out=cwv[:], in_=dr["convw"]), 16)
            k.inc("dli", sy.dma_start(out=cbv[:], in_=dr["convb"]), 16)
            k.inc("dli", sy.dma_start(out=dtbv[:], in_=dr["dtb"]), 16)
            k.inc("dli", sy.dma_start(out=dpv[:], in_=dr["dpar"]), 16)
            k.inc("dwt", ge.dma_start(out=sel[:], in_=dr["selmat"]), 16)
            colmath_stats()
            k.wait("se", "dli")
            k.inc("se", se.activation(acoef[:], alg[:], AF.Exp), 1)
            k.wait("ve", "se")
            ve.tensor_scalar(acoef[:], acoef[:], -1.0, None, ALU.mult)
            ve.memset(hstate[:], 0.0)
            k.inc("ve", ve.memset(xin[:], 0.0), 1)
            k.barrier()

            l1s = int(os.environ.get("KL1S", "9"))
            for tc in range(NTC1):
                t0 = tc * TC1
                for s in range(2):
                    ti = tc * 2 + s
                    buf = xt[s]
                    k.inc(f"dl{s}", sy.dma_start(
                        out=buf[:], in_=out_hbm[ti * 128:(ti + 1) * 128, :]), 16)
                    k.wait("se", f"dl{s}")
                    k.wait("se", "pe")
                    lnd = k.inc("se", se.activation(
                        xlb[:], buf[:], AF.Identity,
                        scale=rcol[:, ti:ti + 1], bias=nmr[:, ti:ti + 1]), 1)
                    k.wait("pe", "se", lnd)
                    k.wait("pe", "ve")
                    ist = None
                    for c in range(8):
                        ist = pe.transpose(psT[:, c * 128:(c + 1) * 128],
                                           xlb[:, c * 128:(c + 1) * 128], idb[:])
                    tpd = k.inc("pe", ist, 1)
                    k.wait("ve", "pe", tpd)
                    lastc = None
                    for c in range(8):
                        lastc = ve.tensor_copy(
                            xTb[:, c * TC1 + s * 128: c * TC1 + s * 128 + 128],
                            psT[:, c * 128:(c + 1) * 128])
                    k.inc("ve", lastc, 1)
                if l1s <= 1:
                    k.barrier()
                    continue
                k.wait("pe", "ve")
                for dc in range(16):
                    pa = psA[dc % 2]
                    k.wait("pe", "ve")
                    last = None
                    for kc in range(8):
                        last = pe.matmul(
                            pa[:, 0:TC1],
                            wxz[:, kc * 2 * DI + dc * 128: kc * 2 * DI + dc * 128 + 128],
                            xTb[:, kc * TC1:(kc + 1) * TC1],
                            start=(kc == 0), stop=(kc == 7))
                    mmd = k.inc("pe", last, 1)
                    k.wait("ve", "pe", mmd)
                    xb = xin[:, dc * (TC1 + 3): (dc + 1) * (TC1 + 3)]
                    ve.tensor_copy(xb[:, 0:3], xb[:, TC1:TC1 + 3])
                    k.inc("ve", ve.tensor_copy(xb[:, 3:3 + TC1], pa[:, 0:TC1]), 1)
                    ve.tensor_scalar_mul(xcs[:], xb[:, 0:TC1], cwv[:, dc * 4: dc * 4 + 1])
                    lastcv = None
                    for tau in range(1, 4):
                        lastcv = ve.scalar_tensor_tensor(
                            xcs[:], xb[:, tau:tau + TC1],
                            cwv[:, dc * 4 + tau: dc * 4 + tau + 1],
                            xcs[:], ALU.mult, ALU.add)
                    cvd = k.inc("ve", lastcv, 1)
                    k.wait("se", "ve", cvd)
                    k.inc("se", se.activation(ubf[:, dc * TC1:(dc + 1) * TC1], xcs[:],
                                              AF.Silu, bias=cbv[:, dc:dc + 1]), 1)
                if l1s <= 2:
                    k.barrier()
                    continue
                for dc in range(16):
                    pa = psA[dc % 2]
                    k.wait("pe", "se")
                    k.wait("pe", "ve")
                    last = None
                    for kc in range(8):
                        last = pe.matmul(
                            pa[:, 0:TC1],
                            wxz[:, kc * 2 * DI + DI + dc * 128:
                                kc * 2 * DI + DI + dc * 128 + 128],
                            xTb[:, kc * TC1:(kc + 1) * TC1],
                            start=(kc == 0), stop=(kc == 7))
                    mmd = k.inc("pe", last, 1)
                    k.wait("se", "pe", mmd)
                    k.inc("se", se.activation(zsb[:, dc * TC1:(dc + 1) * TC1],
                                              pa[:, 0:TC1], AF.Silu), 1)
                if l1s <= 3:
                    k.barrier()
                    continue
                k.wait("pe", "se")
                last = None
                for dc in range(16):
                    last = pe.matmul(ps33[:],
                                     xpt[:, dc * 33:(dc + 1) * 33],
                                     ubf[:, dc * TC1:(dc + 1) * TC1],
                                     start=(dc == 0), stop=(dc == 15))
                mmd = k.inc("pe", last, 1)
                k.wait("se", "pe", mmd)
                se.activation(bcb[:], ps33[:], AF.Identity)
                ssd = k.inc("se", se.activation(dtrb[:], ps33[0:1, :], AF.Identity), 1)
                k.wait("pe", "se", ssd)
                for n in range(NST):
                    pa = psA[n % 2]
                    k.wait("pe", "se")
                    bd = k.inc("pe", pe.matmul(
                        pa[:, 0:TC1], sel[:, (1 + n) * 128:(2 + n) * 128],
                        bcb[:], start=True, stop=True), 1)
                    k.wait("se", "pe", bd)
                    k.inc("se", se.activation(brep[:, n * TC1:(n + 1) * TC1],
                                              pa[:, 0:TC1], AF.Identity), 1)
                for n in range(NST):
                    pa = psA[n % 2]
                    k.wait("pe", "se")
                    bd = k.inc("pe", pe.matmul(
                        pa[:, 0:TC1], sel[:, (17 + n) * 128:(18 + n) * 128],
                        bcb[:], start=True, stop=True), 1)
                    k.wait("se", "pe", bd)
                    ci3 = cint[:].rearrange("p (t n) -> p t n", n=NST)
                    k.inc("se", se.activation(ci3[:, :, n], pa[:, 0:TC1], AF.Identity), 1)
                if l1s <= 4:
                    k.barrier()
                    continue
                for dc in range(16):
                    pa = psA[dc % 2]
                    k.wait("pe", "se")
                    k.wait("pe", "ve")
                    dtd = k.inc("pe", pe.matmul(
                        pa[:, 0:TC1], dtws[0:1, dc * 128:(dc + 1) * 128],
                        dtrb[:], start=True, stop=True), 1)
                    k.wait("ve", "pe", dtd)
                    vrd = k.inc("ve", ve.tensor_scalar(
                        rsc[:], pa[:, 0:TC1], dtbv[:, dc:dc + 1], 0.0,
                        ALU.add, ALU.max), 1)
                    k.wait("se", "pe", dtd)
                    k.wait("se", "ve", vrd)
                    se.activation(dsc[:], pa[:, 0:TC1], AF.Abs, bias=dtbv[:, dc:dc + 1])
                    se.activation(dsc[:], dsc[:], AF.Exp, scale=cn1[:, 0:1])
                    sd = k.inc("se", se.activation(dsc[:], dsc[:], AF.Ln,
                                                   bias=c1[:, 0:1]), 1)
                    k.wait("ve", "se", sd)
                    dd = k.inc("ve", ve.tensor_tensor(dsc[:], dsc[:], rsc[:], ALU.add), 1)
                    dud = k.inc("ve", ve.tensor_tensor(
                        dub[:], dsc[:], ubf[:, dc * TC1:(dc + 1) * TC1], ALU.mult), 1)
                    k.wait("se", "ve", dd)
                    if l1s <= 5:
                        continue
                    hi3 = hint[:].rearrange("p (t n) -> p t n", n=NST)
                    a_used = {}
                    for n in range(NST):
                        jj = n % 2
                        if n >= 2:
                            k.wait("se", "ve", a_used[n - 2])
                        ad = k.inc("se", se.activation(
                            asc[jj][:], dsc[:], AF.Exp,
                            scale=acoef[:, dc * 16 + n: dc * 16 + n + 1]), 1)
                        ve.tensor_tensor(
                            bsc[jj][:], dub[:], brep[:, n * TC1:(n + 1) * TC1],
                            ALU.mult)
                        k.wait("ve", "se", ad)
                        a_used[n] = k.inc("ve", ve.tensor_tensor_scan(
                            hi3[:, :, n], asc[jj][:], bsc[jj][:],
                            hstate[:, dc * 16 + n: dc * 16 + n + 1],
                            ALU.mult, ALU.add), 1)
                    if l1s <= 6:
                        continue
                    ve.tensor_copy(hstate[:, dc * 16:(dc + 1) * 16],
                                   hint[:, (TC1 - 1) * 16: TC1 * 16])
                    ve.tensor_tensor(tmpb[:], hint[:], cint[:], ALU.mult)
                    tm3 = tmpb[:].rearrange("p (t n) -> p t n", n=NST)
                    k.lp_flag = True
                    ve.tensor_reduce(ybuf[:, dc * TC1:(dc + 1) * TC1], tm3,
                                     AX.X, ALU.add)
                    k.lp_flag = False
                    k.inc("ve", ve.scalar_tensor_tensor(
                        ybuf[:, dc * TC1:(dc + 1) * TC1],
                        ubf[:, dc * TC1:(dc + 1) * TC1], dpv[:, dc:dc + 1],
                        ybuf[:, dc * TC1:(dc + 1) * TC1], ALU.mult, ALU.add), 1)
                if l1s <= 7:
                    k.barrier()
                    continue
                md = k.inc("ve", ve.tensor_tensor(mbf[:], ybuf[:], zsb[:], ALU.mult), 1)
                k.wait("pe", "ve", md)
                for sl in range(2):
                    ti = tc * 2 + sl
                    k.wait("pe", "ve")
                    last = None
                    for h in range(2):
                        for dc in range(16):
                            last = pe.matmul(
                                psO[:, h * 512:(h + 1) * 512],
                                mbf[:, dc * TC1 + sl * 128: dc * TC1 + sl * 128 + 128],
                                opw[:, dc * D + h * 512: dc * D + h * 512 + 512],
                                start=(dc == 0), stop=(dc == 15))
                    mmd = k.inc("pe", last, 1)
                    k.wait("ve", "pe", mmd)
                    k.wait("ve", "st0")
                    k.wait("ve", "st1")
                    rd = k.inc("ve", ve.scalar_tensor_tensor(
                        res[sl][:], psO[:], c1[:, 0:1], xt[sl][:],
                        ALU.mult, ALU.add, accum_out=macc[:, ti:ti + 1]), 1)
                    k.wait("se", "ve", rd)
                    k.inc("se", se.activation(dumpt[:], res[sl][:], AF.Square,
                                              accum_out=sqacc[:, ti:ti + 1]), 1)
                    k.wait("sy", "ve", rd)
                    k.inc(f"st{sl}", sy.dma_start(
                        out=out_hbm[ti * 128:(ti + 1) * 128, :], in_=res[sl][:]), 16)
                k.barrier()
            k.flush()

    phases = [
        lambda: phase_l0ffn(0, x_in, first=True),
        lambda: phase_l0attn(),
        lambda: phase_l0ffn(1, out_hbm, first=False),
        lambda: phase_l1ffn(0),
        lambda: phase_l1attn(),
        lambda: phase_l1ffn(1),
    ]
    for ph in phases[:nphases]:
        ph()
    k.wait("sy", "dst")
    k.barrier()
    k.flush()

    bes.close()
    es0.close()
    return nc


def _prep_host(inputs):
    f32 = np.float32
    g = {}
    g["bwt"] = np.ascontiguousarray(np.asarray(inputs["L0_attn_B_weight"], f32).T)
    g["omega"] = np.ascontiguousarray(
        np.asarray(inputs["L0_attn_omega_raw"], f32).reshape(8, 128).T)
    g["wqt"] = np.ascontiguousarray(
        np.asarray(inputs["L0_ffn_weight_q"], np.int8).T
        .reshape(8, 128, D).transpose(1, 0, 2).reshape(128, 8 * D))
    ws = np.asarray(inputs["L0_ffn_weight_scale"], f32)
    g["wsrep"] = np.ascontiguousarray(np.broadcast_to(ws[None, :], (128, D)))
    g["biasrep"] = np.ascontiguousarray(np.broadcast_to(
        np.asarray(inputs["L0_ffn_bias"], f32)[None, :], (128, D)))
    g["wxzt"] = np.ascontiguousarray(np.asarray(inputs["L1_attn_in_proj_w"], f32).T)
    cw = np.asarray(inputs["L1_attn_conv_w"], f32)[:, 0, :]
    g["convw"] = np.ascontiguousarray(
        cw.reshape(16, 128, 4).transpose(1, 0, 2).reshape(128, 64))
    g["convb"] = np.ascontiguousarray(
        np.asarray(inputs["L1_attn_conv_b"], f32).reshape(16, 128).T)
    g["dtb"] = np.ascontiguousarray(
        np.asarray(inputs["L1_attn_dt_b"], f32).reshape(16, 128).T)
    g["dpar"] = np.ascontiguousarray(
        np.asarray(inputs["L1_attn_D_param"], f32).reshape(16, 128).T)
    g["xpt"] = np.ascontiguousarray(np.asarray(inputs["L1_attn_x_proj_w"], f32).T)
    g["dtw"] = np.ascontiguousarray(
        np.asarray(inputs["L1_attn_dt_w"], f32)[:, 0].reshape(1, 2048))
    g["alog"] = np.ascontiguousarray(
        np.asarray(inputs["L1_attn_A_log"], f32).reshape(16, 128, 16)
        .transpose(1, 0, 2).reshape(128, 256))
    g["opwt"] = np.ascontiguousarray(np.asarray(inputs["L1_attn_out_proj_w"], f32).T)
    g["cw1"] = np.ascontiguousarray(np.asarray(inputs["L1_ffn_cn_w1"], f32))
    g["cb1"] = np.ascontiguousarray(
        np.asarray(inputs["L1_ffn_cn_b1"], f32).reshape(8, 128).T)
    g["cw2"] = np.ascontiguousarray(np.asarray(inputs["L1_ffn_cn_w2"], f32))
    g["cb2"] = np.ascontiguousarray(
        np.asarray(inputs["L1_ffn_cn_b2"], f32).reshape(4, 128).T)
    g["cw3"] = np.ascontiguousarray(np.asarray(inputs["L1_ffn_cn_w3"], f32))
    g["cb3"] = np.ascontiguousarray(
        np.asarray(inputs["L1_ffn_cn_b3"], f32).reshape(2, 128).T)
    g["w4rep"] = np.ascontiguousarray(
        np.broadcast_to(np.asarray(inputs["L1_ffn_cn_w4"], f32), (256, 128)))
    g["b4"] = np.asarray(inputs["L1_ffn_cn_b4"], f32).reshape(1, 1)
    g["gw1"] = np.ascontiguousarray(np.asarray(inputs["L1_ffn_gc_w1"], f32))
    g["gb1"] = np.ascontiguousarray(
        np.asarray(inputs["L1_ffn_gc_b1"], f32).reshape(4, 128).T)
    g["gw2"] = np.ascontiguousarray(np.asarray(inputs["L1_ffn_gc_w2"], f32))
    g["gb2"] = np.ascontiguousarray(
        np.asarray(inputs["L1_ffn_gc_b2"], f32).reshape(2, 128).T)
    g["gw3"] = np.ascontiguousarray(
        np.asarray(inputs["L1_ffn_gc_w3"], f32).reshape(2, 128).T)
    g["gb3"] = np.asarray(inputs["L1_ffn_gc_b3"], f32).reshape(1, 1)
    g["arange_"] = np.asarray(inputs["L1_ffn_adaptive_range"], f32).reshape(1, 1)
    g["identbf"] = np.eye(128).astype(ml_dtypes.bfloat16)
    selm = np.zeros((33, 33, 128), np.float32)
    for r in range(33):
        selm[r, r, :] = 1.0
    g["selmat"] = np.ascontiguousarray(selm.reshape(33, 33 * 128))
    g["identf"] = np.eye(128).astype(f32)
    return g


def kernel(nphases=6, dbg=False, **inputs):
    shared = _prep_host(inputs)
    x = np.asarray(inputs["L0_attn_x_seq"], np.float32)
    in_maps = []
    for b in range(8):
        m = dict(shared)
        m["x"] = np.ascontiguousarray(x[b])
        in_maps.append(m)
    nc = build_nc(nphases, dbg)
    res = run_bass_kernel_spmd(nc, in_maps, core_ids=list(range(8)))
    return np.stack([np.asarray(r["out"], np.float32) for r in res.results], axis=0)


if __name__ == "__main__":
    nc = build_nc()
    n = sum(1 for _ in nc.all_instructions())
    print("built ok, instructions:", n)


# version 06 (final)
# speedup vs baseline: 1.0311x; 1.0311x over previous
import os
import numpy as np
import ml_dtypes
import concourse.bass as bass
import concourse.mybir as mybir
from concourse.bass_utils import run_bass_kernel_spmd
from contextlib import ExitStack

F32 = mybir.dt.float32
BF16 = mybir.dt.bfloat16
I8 = mybir.dt.int8
I32 = mybir.dt.int32
AF = mybir.ActivationFunctionType
ALU = mybir.AluOpType
AX = mybir.AxisListType

T = 4096
D = 1024
NT = T // 128
DI = 2048
NST = 16
C0 = 512
NC0 = T // C0
TC1 = 256
NTC1 = T // TC1
TCF = 512
NTCF = T // TCF
DT = 0.1
DT2 = 0.01

SEMS = ["pe", "ve", "se", "ge", "sy", "dli", "dl0", "dl1", "st0", "st1", "dst", "dwt", "fence"]
ENG_ATTR = {"pe": "tensor", "ve": "vector", "se": "scalar",
            "ge": "gpsimd", "sy": "sync"}


class Rec:
    __slots__ = ("meth", "a", "kw", "incs", "lp", "fs")

    def __init__(self, meth, a, kw, lp):
        self.meth = meth
        self.a = a
        self.kw = kw
        self.incs = []
        self.lp = lp
        self.fs = 1 << 30

    def then_inc(self, sem, n):
        self.incs.append((sem, n))
        return self


class EngRec:
    def __init__(self, ctx, prog):
        object.__setattr__(self, "_ctx", ctx)
        object.__setattr__(self, "_prog", prog)
        object.__setattr__(self, "_narrow", False)

    _NONCOMPUTE = ("wait_ge", "nop", "engine_nop", "dma_start")

    @staticmethod
    def _free_sz(a):
        for x in a:
            shp = getattr(x, "shape", None)
            if shp is not None and len(shp) >= 2:
                n = 1
                for d in shp[1:]:
                    n *= d
                return n
        return 1 << 30

    def __getattr__(self, meth):
        def call(*a, **kw):
            if self._prog:
                prev = self._prog[-1]
                if meth == "transpose" and prev.meth == "transpose":
                    self._prog.append(Rec("wait_ge",
                                          (self._ctx.sem["fence"], 1), {}, False))
                elif (self._narrow and meth not in self._NONCOMPUTE
                      and prev.meth not in self._NONCOMPUTE
                      and prev.meth != "drain"
                      and (self._free_sz(a) < 128 or prev.fs < 128)):
                    self._prog.append(Rec("drain", (), {}, False))
            r = Rec(meth, a, kw, self._ctx.lp_flag)
            r.fs = self._free_sz(a) if meth not in self._NONCOMPUTE else (1 << 30)
            self._prog.append(r)
            return r
        return call


class Ctx:
    def __init__(self, nc, block, semh):
        self.nc = nc
        self.block = block
        self.sem = semh
        self.cnt = {n: 0 for n in semh}
        self.last_wait = {}
        self.lp_flag = False
        self.prog = {n: [] for n in ENG_ATTR}
        self.e = {}
        for n in ENG_ATTR:
            er = EngRec(self, self.prog[n])
            er._narrow = n in ("ve", "se", "ge")
            self.e[n] = er

    def inc(self, name, instr, n=1):
        instr.then_inc(self.sem[name], n)
        self.cnt[name] += n
        return self.cnt[name]

    def wait(self, engname, semname, val=None):
        if val is None:
            val = self.cnt[semname]
        if val <= 0:
            return
        key = (engname, semname)
        if self.last_wait.get(key, -1) < val:
            self.e[engname].wait_ge(self.sem[semname], val)
            self.last_wait[key] = val

    def barrier(self):
        for en, sn in (("pe", "pe"), ("ve", "ve"), ("se", "se"),
                       ("ge", "ge"), ("sy", "sy")):
            self.e[en].drain()
            self.inc(sn, self.e[en].nop(), 1)
        for en in ("pe", "ve", "se", "ge", "sy"):
            for sn in SEMS:
                self.wait(en, sn)

    def flush(self):
        nc = self.nc
        knop = {}
        for part in os.environ.get("KNOP", "").split(","):
            if ":" in part:
                en, rng_ = part.split(":")
                a_, b_ = rng_.split("-")
                knop.setdefault(en, []).append((int(a_), int(b_)))
        if not hasattr(self, "opidx"):
            self.opidx = {n: 0 for n in ENG_ATTR}
        for name, attr in ENG_ATTR.items():
            prog = list(self.prog[name])
            self.prog[name].clear()
            if not prog:
                continue
            base = self.opidx[name]
            self.opidx[name] += len(prog)

            def section(e, prog=prog, name=name, base=base):
                for idx, r in enumerate(prog):
                    gi = base + idx
                    killwaits = os.environ.get("KNOPW", "")
                    skipm = ("nop", "engine_nop", "dma_start")
                    if name not in killwaits:
                        skipm = skipm + ("wait_ge",)
                    dead = (name in knop
                            and any(a_ <= gi < b_ for a_, b_ in knop[name])
                            and r.meth not in skipm)
                    if dead:
                        inst = e.nop()
                    elif r.lp:
                        with nc.allow_low_precision(reason="bf16 accum"):
                            inst = getattr(e, r.meth)(*r.a, **r.kw)
                    else:
                        inst = getattr(e, r.meth)(*r.a, **r.kw)
                    for s, n in r.incs:
                        inst.then_inc(s, n)
            getattr(self.block, attr)(section)


def build_nc(nphases=6, dbg=False):
    nc = bass.Bass("TRN2", target_bir_lowering=False, detect_race_conditions=False)
    dr = {}

    def din(name, shape, dt=F32):
        dr[name] = nc.dram_tensor(name, shape, dt, kind="ExternalInput").ap()
        return dr[name]

    x_in = din("x", [T, D])
    out_hbm = nc.dram_tensor("out", [T, D], F32, kind="ExternalOutput").ap()
    dbg1 = nc.dram_tensor("dbg1", [128, D], F32, kind="ExternalOutput").ap()
    dbg2 = nc.dram_tensor("dbg2", [128, D], F32, kind="ExternalOutput").ap()
    dbg3 = nc.dram_tensor("dbg3", [128, D], F32, kind="ExternalOutput").ap()
    dbg4 = nc.dram_tensor("dbg4", [128, NT], F32, kind="ExternalOutput").ap()
    dbg5 = nc.dram_tensor("dbg5", [128, 8 * C0], F32, kind="ExternalOutput").ap()
    dbg6 = nc.dram_tensor("dbg6", [128, 32], F32, kind="ExternalOutput").ap()
    dbg7 = nc.dram_tensor("dbg7", [128, 8 * C0], F32, kind="ExternalOutput").ap()
    dbg8 = nc.dram_tensor("dbg8", [128, 8 * T], F32, kind="ExternalOutput").ap()
    dbg9 = nc.dram_tensor("dbg9", [128, T], F32, kind="ExternalOutput").ap()
    dbg10 = nc.dram_tensor("dbg10", [128, 16], F32, kind="ExternalOutput").ap()

    din("bwt", [D, D]); din("omega", [128, 8])
    din("wqt", [128, 8 * D], I8); din("wsrep", [128, D]); din("biasrep", [128, D])
    din("wxzt", [D, 2 * DI]); din("convw", [128, 64]); din("convb", [128, 16])
    din("dtb", [128, 16]); din("dpar", [128, 16]); din("xpt", [DI, 33])
    din("dtw", [1, 2048]); din("alog", [128, 256]); din("opwt", [DI, D])
    din("cw1", [D, D]); din("cb1", [128, 8]); din("cw2", [D, 512]); din("cb2", [128, 4])
    din("cw3", [512, 256]); din("cb3", [128, 2]); din("w4rep", [256, 128]); din("b4", [1, 1])
    din("gw1", [D, 512]); din("gb1", [128, 4]); din("gw2", [512, 256]); din("gb2", [128, 2])
    din("gw3", [128, 2]); din("gb3", [1, 1]); din("arange_", [1, 1])
    din("identbf", [128, 128], BF16); din("identf", [128, 128], F32)
    din("selmat", [33, 33 * 128])

    es0 = ExitStack()

    def sb(name, shape, dt=F32):
        return es0.enter_context(nc.sbuf_tensor(name, shape, dt))

    idb = sb("idb", [128, 128], BF16)
    idf = sb("idf", [128, 128], F32)
    onesrow = sb("onesrow", [1, 128], F32)
    onesrow_b = sb("onesrow_b", [1, 128], BF16)
    c1 = sb("c1", [128, 1], F32)
    cn1 = sb("cn1", [128, 1], F32)
    chalf = sb("chalf", [128, 1], F32)
    ceps = sb("ceps", [128, 1], F32)
    macc = sb("macc", [128, NT], F32)
    sqacc = sb("sqacc", [128, NT], F32)
    mcol = sb("mcol", [128, NT], F32)
    rcol = sb("rcol", [128, NT], F32)
    nmr = sb("nmr", [128, NT], F32)
    scrN = sb("scrN", [128, NT], F32)
    dumpt = sb("dumpt", [128, 1024], BF16)

    bes = ExitStack()
    semh = {n: bes.enter_context(nc.semaphore(n)) for n in SEMS}
    block = bes.enter_context(nc.Block())
    k = Ctx(nc, block, semh)
    pe, ve, se, ge, sy = k.e["pe"], k.e["ve"], k.e["se"], k.e["ge"], k.e["sy"]

    k.inc("dli", sy.dma_start(out=idb[:], in_=dr["identbf"]), 16)
    k.inc("dli", sy.dma_start(out=idf[:], in_=dr["identf"]), 16)
    ve.memset(onesrow[:], 1.0)
    ve.memset(onesrow_b[:], 1.0)
    ve.memset(c1[:], 1.0)
    ve.memset(cn1[:], -1.0)
    ve.memset(chalf[:], 0.5)
    k.inc("fence", ve.memset(ceps[:], 1e-5), 1)
    k.inc("ve", ve.memset(ceps[:], 1e-5), 1)
    k.barrier()
    k.flush()

    def colmath_stats():
        ve.tensor_scalar(mcol[:], macc[:], 1.0 / D, None, ALU.mult)
        ve.tensor_scalar(scrN[:], sqacc[:], 1.0 / D, None, ALU.mult)
        ve.tensor_tensor(rcol[:], mcol[:], mcol[:], ALU.mult)
        k.inc("ve", ve.tensor_tensor(scrN[:], scrN[:], rcol[:], ALU.subtract), 1)
        k.wait("se", "ve")
        k.inc("se", se.activation(scrN[:], scrN[:], AF.Sqrt, bias=ceps[:, 0:1]), 1)
        k.wait("ve", "se")
        ve.reciprocal(rcol[:], scrN[:])
        ve.tensor_tensor(nmr[:], mcol[:], rcol[:], ALU.mult)
        k.inc("ve", ve.tensor_scalar(nmr[:], nmr[:], -1.0, None, ALU.mult), 1)
        k.wait("se", "ve")
        k.wait("sy", "ve")

    def phase_l0ffn(app, src_hbm, first):
        es = ExitStack()

        def p(name, shape, dt=F32):
            return es.enter_context(nc.sbuf_tensor(f"f{app}_{name}", shape, dt))

        with es:
            wq_i8 = p("wq8", [128, 8 * D], I8)
            wq = p("wq", [128, 8 * D], BF16)
            wsr = p("wsr", [128, D], F32)
            bh = p("bh", [128, D], F32)
            xt = [p(f"xt{j}", [128, D], F32) for j in range(2)]
            xln = [p(f"xln{j}", [128, D], F32) for j in range(2)]
            qs = p("qs", [128, D], F32)
            q0b = p("q0b", [128, D], BF16)
            q0i = p("q0i", [128, D], I32)
            uu = p("uu", [128, D], F32)
            cc1 = p("cc1", [128, D], F32)
            qb = p("qb", [128, D], BF16)
            qT = [p(f"qT{j}", [128, D], BF16) for j in range(2)]
            acol = p("acol", [128, 2], F32)
            cqcol = p("cqcol", [128, 2], F32)
            xshc = p("xshc", [128, 2], F32)
            u1 = p("u1", [128, D], F32)
            u2 = p("u2", [128, D], F32)
            res = [p(f"res{j}", [128, D], F32) for j in range(2)]
            psT = es.enter_context(nc.psum_tensor(f"f{app}_psT", [128, 1024], BF16))
            psM = [es.enter_context(nc.psum_tensor(f"f{app}_psM{j}", [128, 512], F32))
                   for j in range(4)]

            k.inc("dli", sy.dma_start(out=wq_i8[:], in_=dr["wqt"]), 16)
            k.inc("dli", sy.dma_start(out=wsr[:], in_=dr["wsrep"]), 16)
            k.inc("dli", sy.dma_start(out=bh[:], in_=dr["biasrep"]), 16)
            k.wait("ve", "dwt")
            k.wait("ve", "dli")
            k.inc("ve", ve.tensor_copy(wq[:], wq_i8[:]), 1)
            k.inc("ve", ve.tensor_scalar(bh[:], bh[:], 0.5, None, ALU.mult), 1)

            trunc = os.environ.get("KTRUNC", "")
            skip_pre = trunc.endswith("np")
            if skip_pre:
                trunc = trunc[:-2]
            if first and not skip_pre:
                sdone = {}
                for i in range(NT):
                    j = i % 2
                    if i >= 2:
                        k.wait("sy", "se", sdone[i - 2])
                    k.inc(f"dl{j}", sy.dma_start(
                        out=xt[j][:], in_=src_hbm[i * 128:(i + 1) * 128, :]), 16)
                    k.wait("se", f"dl{j}")
                    se.activation(dumpt[:], xt[j][:], AF.Identity,
                                  accum_out=macc[:, i:i + 1])
                    sdone[i] = k.inc("se", se.activation(
                        dumpt[:], xt[j][:], AF.Square,
                        accum_out=sqacc[:, i:i + 1]), 1)
                k.barrier()
            colmath_stats()
            if trunc == "prepass":
                k.barrier()
                k.flush()
                return

            res_done = {}
            tpq_done = {}
            evac_done = {}
            qt_copied = {}
            mm_of = {}
            qbd_of = {}
            ntl = NT
            stage = 9
            if trunc.startswith("s"):
                parts = trunc[1:].split("-")
                stage = int(parts[0])
                ntl = int(parts[1]) if len(parts) > 1 else 2
            elif trunc.isdigit():
                ntl = int(trunc)
            for i in range(ntl):
                j = i % 2
                if i >= 2:
                    k.wait("sy", "ve", res_done[i - 2])
                k.inc(f"dl{j}", sy.dma_start(
                    out=xt[j][:], in_=src_hbm[i * 128:(i + 1) * 128, :]), 16)
                k.wait("se", f"dl{j}")
                if i >= 2:
                    k.wait("se", "ve", qbd_of[i - 2])
                lnd = k.inc("se", se.activation(
                    xln[j][:], xt[j][:], AF.Identity,
                    scale=rcol[:, i:i + 1], bias=nmr[:, i:i + 1]), 1)
                k.wait("ve", "se", lnd)
                ve.tensor_reduce(acol[:, j:j + 1], xln[j][:], AX.X, ALU.max,
                                 apply_absolute_value=True)
                ve.tensor_scalar(acol[:, j:j + 1], acol[:, j:j + 1], 1e-6,
                                 0.5 / 127.0, ALU.max, ALU.mult)
                ve.reciprocal(cqcol[:, j:j + 1], acol[:, j:j + 1])
                ve.tensor_scalar(cqcol[:, j:j + 1], cqcol[:, j:j + 1], 0.5, None,
                                 ALU.mult)
                ve.tensor_copy(xshc[:, j:j + 1], acol[:, j:j + 1])
                ve.tensor_scalar_mul(qs[:], xln[j][:], cqcol[:, j:j + 1])
                ve.tensor_copy(q0i[:], qs[:])
                ve.tensor_copy(q0b[:], q0i[:])
                ve.tensor_tensor(uu[:], qs[:], q0b[:], ALU.subtract)
                ve.tensor_scalar(cc1[:], uu[:], 0.5, None, ALU.is_ge)
                ve.tensor_scalar(uu[:], uu[:], -0.5, None, ALU.is_le)
                ve.tensor_tensor(cc1[:], cc1[:], uu[:], ALU.subtract)
                if i >= 1:
                    k.wait("ve", "pe", tpq_done[i - 1])
                qbd_of[i] = qbd = k.inc("ve", ve.tensor_tensor(
                    qb[:], q0b[:], cc1[:], ALU.add), 1)
                if stage <= 1:
                    continue
                k.wait("pe", "ve", qbd)
                if i >= 1:
                    k.wait("pe", "ve", qt_copied[i - 1])
                ist = None
                for c in range(8):
                    ist = pe.transpose(psT[:, c * 128:(c + 1) * 128],
                                       qb[:, c * 128:(c + 1) * 128], idb[:])
                tpq_done[i] = k.inc("pe", ist, 1)
                k.wait("ve", "pe", tpq_done[i])
                if i >= 2:
                    k.wait("ve", "pe", mm_of[i - 2])
                qt_copied[i] = qtc = k.inc("ve", ve.tensor_copy(qT[j][:], psT[:]), 1)
                if stage <= 2:
                    continue
                k.wait("pe", "ve", qtc)
                mb = [psM[2 * j], psM[2 * j + 1]]
                if i >= 2:
                    k.wait("pe", "se", evac_done[i - 2])
                last = None
                for h in range(2):
                    for kc in range(8):
                        last = pe.matmul(mb[h][:],
                                         qT[j][:, kc * 128:(kc + 1) * 128],
                                         wq[:, kc * D + h * 512: kc * D + h * 512 + 512],
                                         start=(kc == 0), stop=(kc == 7))
                mm_of[i] = mmd = k.inc("pe", last, 1)
                if stage <= 3:
                    continue
                if os.environ.get("KEVAC", "") == "v":
                    k.wait("ve", "pe", mmd)
                    ve.tensor_scalar_mul(u1[:, 0:512], mb[0][:], xshc[:, j:j + 1])
                    evac_done[i] = k.inc("ve", ve.tensor_scalar_mul(
                        u1[:, 512:1024], mb[1][:], xshc[:, j:j + 1]), 1)
                    k.wait("se", "ve", evac_done[i])
                    k.inc("se", se.engine_nop() if False else se.activation(
                        dumpt[:, 0:1], xshc[:, 0:1], AF.Identity), 1)
                    evac_done[i] = k.cnt["se"]
                else:
                    k.wait("se", "pe", mmd)
                    se.activation(u1[:, 0:512], mb[0][:], AF.Identity,
                                  scale=xshc[:, j:j + 1])
                    evac_done[i] = k.inc("se", se.activation(
                        u1[:, 512:1024], mb[1][:], AF.Identity,
                        scale=xshc[:, j:j + 1]), 1)
                if stage <= 4:
                    continue
                k.wait("ve", "se", evac_done[i])
                ve.tensor_tensor(u2[:], u1[:], wsr[:], ALU.mult)
                u2d = ve.tensor_tensor(u2[:], u2[:], bh[:], ALU.add)
                if stage <= 5:
                    k.inc("ve", u2d, 1)
                    continue
                if i >= 2:
                    k.wait("ve", f"st{j}")
                res_done[i] = k.inc("ve", ve.scalar_tensor_tensor(
                    res[j][:], u2[:], c1[:, 0:1], xt[j][:], ALU.mult, ALU.add,
                    accum_out=macc[:, i:i + 1]), 1)
                if stage <= 6:
                    continue
                k.wait("se", "ve", res_done[i])
                k.inc("se", se.activation(dumpt[:], res[j][:], AF.Square,
                                          accum_out=sqacc[:, i:i + 1]), 1)
                if stage <= 7:
                    continue
                k.wait("sy", "ve", res_done[i])
                k.inc(f"st{j}", sy.dma_start(
                    out=out_hbm[i * 128:(i + 1) * 128, :], in_=res[j][:]), 16)
                if dbg and app == 0 and i == 0:
                    k.wait("sy", "se")
                    k.inc("dst", sy.dma_start(out=dbg1, in_=xln[0][:]), 16)
                    k.inc("dst", sy.dma_start(out=dbg2, in_=u1[:]), 16)
                    k.inc("dst", sy.dma_start(out=dbg3, in_=res[j][:]), 16)
            if dbg and app == 0:
                k.inc("dst", sy.dma_start(out=dbg4[0:128, 0:NT], in_=rcol[:]), 16)
            k.barrier()
            k.flush()

    def phase_l0attn():
        es = ExitStack()

        def p(name, shape, dt=F32):
            return es.enter_context(nc.sbuf_tensor(f"a0_{name}", shape, dt))

        with es:
            bwt = p("bwt", [128, 8 * D], BF16)
            omg = p("omg", [128, 8], F32)
            sfp = p("sfp", [128, 8], F32)
            uom = p("uom", [128, 8], F32)
            ctc = p("ctc", [128, 8], F32)
            s2c = p("s2c", [128, 8], F32)
            t8a = p("t8a", [128, 8], F32)
            t8b = p("t8b", [128, 8], F32)
            ctab = p("ctab", [128, 8 * (C0 + 2)], F32)
            ptab = p("ptab", [128, 8 * (C0 + 2)], F32)
            su = p("su", [128, 8], F32)
            sv = p("sv", [128, 8], F32)
            av = p("av", [128, 8], F32)
            au = p("au", [128, 8], F32)
            nsv = p("nsv", [128, 8], F32)
            pqe = p("pqe", [128, 16], F32)
            xt = [p(f"xt{j}", [128, D], F32) for j in range(8)]
            xlb = p("xlb", [128, D], BF16)
            xT = p("xT", [128, 8 * C0], BF16)
            bx = p("bx", [128, 8 * C0], F32)
            w1 = p("w1", [128, C0], F32)
            w2 = p("w2", [128, C0], F32)
            ppv = p("ppv", [128, C0], F32)
            qqv = p("qqv", [128, C0], F32)
            onesC = p("onesC", [128, C0], F32)
            t4 = p("t4", [128, C0], F32)
            yv = p("yv", [128, 8 * C0], F32)
            res = [p(f"res{j}", [128, D], F32) for j in range(2)]
            psT = es.enter_context(nc.psum_tensor("a0_psT", [128, 1024], BF16))
            psB = [es.enter_context(nc.psum_tensor(f"a0_psB{j}", [128, C0], F32))
                   for j in range(2)]
            psY = [es.enter_context(nc.psum_tensor(f"a0_psY{j}", [128, 1024], F32))
                   for j in range(2)]

            ve.memset(onesC[:], 1.0)
            colmath_stats()
            k.inc("dwt", ge.dma_start(
                out=bwt[:].rearrange("p (c o) -> p c o", c=8),
                in_=dr["bwt"].rearrange("(c p) o -> p c o", p=128)), 16)
            k.inc("dli", sy.dma_start(out=omg[:], in_=dr["omega"]), 16)
            k.wait("se", "dli")
            se.activation(sfp[:], omg[:], AF.Abs)
            se.activation(sfp[:], sfp[:], AF.Exp, scale=cn1[:, 0:1])
            spd = k.inc("se", se.activation(sfp[:], sfp[:], AF.Ln, bias=c1[:, 0:1]), 1)
            k.wait("ve", "se", spd)
            k.wait("ve", "dli")
            ve.tensor_scalar(omg[:], omg[:], 0.0, None, ALU.max)
            ve.tensor_tensor(omg[:], omg[:], sfp[:], ALU.add)
            ve.tensor_scalar(uom[:], omg[:], 0.005, None, ALU.mult)
            ve.tensor_scalar(ctc[:], uom[:], -1.0, 1.0, ALU.mult, ALU.add)
            ve.tensor_scalar(s2c[:], uom[:], -1.0, 2.0, ALU.mult, ALU.add)
            ve.tensor_tensor(s2c[:], s2c[:], uom[:], ALU.mult)
            M = C0 + 2
            for fc in range(8):
                base = fc * M

                def cv(a, b):
                    return ctab[:, base + a: base + b]

                def pv(a, b):
                    return ptab[:, base + a: base + b]

                ve.memset(cv(0, 1), 1.0)
                ve.memset(pv(0, 1), 0.0)
                ve.tensor_copy(cv(1, 2), ctc[:, fc:fc + 1])
                ve.memset(pv(1, 2), 1.0)
                n = 2
                while n < M:
                    kk = min(n - 1, M - n)
                    cm = ctab[:, base + n - 1: base + n]
                    pm = ptab[:, base + n - 1: base + n]
                    ve.tensor_tensor(t8a[:, 0:1], s2c[:, fc:fc + 1], pm, ALU.mult)
                    ve.tensor_scalar_mul(cv(n, n + kk), pv(1, 1 + kk), t8a[:, 0:1])
                    ve.scalar_tensor_tensor(cv(n, n + kk), cv(1, 1 + kk), cm[:, 0:1],
                                            cv(n, n + kk), ALU.mult, ALU.subtract)
                    ve.tensor_scalar_mul(pv(n, n + kk), cv(1, 1 + kk), pm[:, 0:1])
                    ve.scalar_tensor_tensor(pv(n, n + kk), pv(1, 1 + kk), cm[:, 0:1],
                                            pv(n, n + kk), ALU.mult, ALU.add)
                    n += kk
            ve.memset(su[:], 0.0)
            ve.memset(sv[:], 0.0)
            ve.memset(nsv[:], 0.0)
            ve.memset(av[:], 0.0)
            k.inc("ve", ve.memset(au[:], 0.0), 1)
            k.barrier()

            for tc in range(NC0):
                for s in range(4):
                    ti = tc * 4 + s
                    buf = xt[ti % 8]
                    k.inc(f"dl{ti % 2}", sy.dma_start(
                        out=buf[:], in_=out_hbm[ti * 128:(ti + 1) * 128, :]), 16)
                    k.wait("se", f"dl{ti % 2}")
                    k.wait("se", "pe")
                    lnd = k.inc("se", se.activation(
                        xlb[:], buf[:], AF.Identity,
                        scale=rcol[:, ti:ti + 1], bias=nmr[:, ti:ti + 1]), 1)
                    k.wait("pe", "se", lnd)
                    k.wait("pe", "ve")
                    ist = None
                    for c in range(8):
                        ist = pe.transpose(psT[:, c * 128:(c + 1) * 128],
                                           xlb[:, c * 128:(c + 1) * 128], idb[:])
                    tpd = k.inc("pe", ist, 1)
                    k.wait("ve", "pe", tpd)
                    lastc = None
                    for c in range(8):
                        lastc = ve.tensor_copy(
                            xT[:, c * C0 + s * 128: c * C0 + s * 128 + 128],
                            psT[:, c * 128:(c + 1) * 128])
                    k.inc("ve", lastc, 1)
                k.wait("pe", "ve")
                bx_ev = {}
                for ec in range(8):
                    pb = psB[ec % 2]
                    if ec >= 2:
                        k.wait("pe", "se", bx_ev[ec - 2])
                    last = None
                    for dc in range(8):
                        last = pe.matmul(
                            pb[:], bwt[:, dc * D + ec * 128: dc * D + ec * 128 + 128],
                            xT[:, dc * C0:(dc + 1) * C0],
                            start=(dc == 0), stop=(dc == 7))
                    mmd = k.inc("pe", last, 1)
                    k.wait("se", "pe", mmd)
                    bx_ev[ec] = k.inc("se", se.activation(
                        bx[:, ec * C0:(ec + 1) * C0], pb[:], AF.Identity), 1)
                k.wait("ve", "se")
                for fc in range(8):
                    tb = fc * M
                    bxs = bx[:, fc * C0:(fc + 1) * C0]
                    ve.scalar_tensor_tensor(w1[:], bxs, chalf[0:128, 0:1], ctab[:, tb:tb + C0],
                                            ALU.bypass, ALU.mult)
                    ve.tensor_scalar(w1[:], w1[:], DT2, None, ALU.mult)
                    ve.scalar_tensor_tensor(w2[:], bxs, chalf[0:128, 0:1], ptab[:, tb:tb + C0],
                                            ALU.bypass, ALU.mult)
                    ve.tensor_scalar(w2[:], w2[:], DT2, None, ALU.mult)
                    ve.tensor_tensor_scan(ppv[:], onesC[:], w1[:], 0.0, ALU.mult, ALU.add)
                    ve.tensor_tensor_scan(qqv[:], onesC[:], w2[:], 0.0, ALU.mult, ALU.add)
                    ys = yv[:, fc * C0:(fc + 1) * C0]
                    ve.scalar_tensor_tensor(ys, ppv[:], av[:, fc:fc + 1],
                                            ptab[:, tb + 1:tb + 1 + C0],
                                            ALU.add, ALU.mult)
                    ve.scalar_tensor_tensor(ys, ptab[:, tb:tb + C0], nsv[:, fc:fc + 1],
                                            ys, ALU.mult, ALU.add)
                    ve.tensor_tensor(t4[:], qqv[:], ctab[:, tb + 1:tb + 1 + C0], ALU.mult)
                    ve.tensor_tensor(ys, ys, t4[:], ALU.subtract)
                    ve.tensor_copy(pqe[:, fc:fc + 1], ppv[:, C0 - 1:C0])
                    ve.tensor_copy(pqe[:, 8 + fc:9 + fc], qqv[:, C0 - 1:C0])
                EE = C0 - 1
                ct3 = ctab[:].rearrange("p (c m) -> p c m", m=M)
                pt3 = ptab[:].rearrange("p (c m) -> p c m", m=M)
                pE = pt3[:, :, EE]
                pE1 = pt3[:, :, EE + 1]
                cE = ct3[:, :, EE]
                cE1 = ct3[:, :, EE + 1]
                PEc = pqe[:, 0:8]
                QEc = pqe[:, 8:16]
                ve.tensor_tensor(t8a[:], pE1, av[:], ALU.mult)
                ve.tensor_tensor(t8b[:], pE, sv[:], ALU.mult)
                ve.tensor_tensor(t8a[:], t8a[:], t8b[:], ALU.subtract)
                ve.tensor_tensor(t8b[:], pE1, PEc, ALU.mult)
                ve.tensor_tensor(t8a[:], t8a[:], t8b[:], ALU.add)
                ve.tensor_tensor(t8b[:], cE1, QEc, ALU.mult)
                ve.tensor_tensor(t8a[:], t8a[:], t8b[:], ALU.subtract)
                ve.tensor_tensor(uom[:], pE1, au[:], ALU.mult)
                ve.tensor_tensor(t8b[:], pE, su[:], ALU.mult)
                ve.tensor_tensor(uom[:], uom[:], t8b[:], ALU.subtract)
                ve.tensor_tensor(t8b[:], pE1, PEc, ALU.mult)
                ve.tensor_tensor(sfp[:], pE, PEc, ALU.mult)
                ve.tensor_tensor(t8b[:], t8b[:], sfp[:], ALU.subtract)
                ve.scalar_tensor_tensor(uom[:], t8b[:], 10.0, uom[:], ALU.mult, ALU.add)
                ve.tensor_tensor(t8b[:], cE1, QEc, ALU.mult)
                ve.tensor_tensor(sfp[:], cE, QEc, ALU.mult)
                ve.tensor_tensor(t8b[:], t8b[:], sfp[:], ALU.subtract)
                ve.scalar_tensor_tensor(uom[:], t8b[:], -10.0, uom[:], ALU.mult, ALU.add)
                ve.tensor_copy(sv[:], t8a[:])
                ve.tensor_copy(su[:], uom[:])
                ve.tensor_scalar(nsv[:], sv[:], -1.0, None, ALU.mult)
                ve.tensor_scalar(t8a[:], ctc[:], 2.0, -1.0, ALU.mult, ALU.add)
                ve.tensor_tensor(av[:], t8a[:], sv[:], ALU.mult)
                ve.scalar_tensor_tensor(av[:], su[:], DT, av[:], ALU.mult, ALU.add)
                ve.tensor_scalar(t8a[:], ctc[:], -20.0, 20.0, ALU.mult, ALU.add)
                ve.tensor_tensor(au[:], t8a[:], sv[:], ALU.mult)
                k.inc("ve", ve.tensor_tensor(au[:], su[:], au[:], ALU.subtract), 1)
                if dbg and tc == 0:
                    k.wait("sy", "ve")
                    k.wait("sy", "se")
                    k.inc("dst", sy.dma_start(out=dbg8[0:128, 0:8 * (C0 + 2)],
                                              in_=ptab[:]), 16)
                    k.inc("dst", sy.dma_start(out=dbg9[0:128, 0:8 * (C0 + 2) - 128],
                                              in_=ctab[:, 0:8 * (C0 + 2) - 128]), 16)
                    k.inc("dst", sy.dma_start(out=dbg5, in_=yv[:]), 16)
                    k.inc("dst", sy.dma_start(out=dbg7, in_=bx[:]), 16)
                    k.inc("dst", sy.dma_start(out=dbg6[0:128, 0:8], in_=su[:]), 16)
                    k.inc("dst", sy.dma_start(out=dbg6[0:128, 8:16], in_=sv[:]), 16)
                    k.inc("dst", sy.dma_start(out=dbg6[0:128, 16:24], in_=av[:]), 16)
                    k.inc("dst", sy.dma_start(out=dbg6[0:128, 24:32], in_=au[:]), 16)
                    k.wait("sy", "dst")
                for s in range(4):
                    ti = tc * 4 + s
                    py = psY[s % 2]
                    k.wait("pe", "ve")
                    last = None
                    for c in range(8):
                        last = pe.transpose(py[:, c * 128:(c + 1) * 128],
                                            yv[:, c * C0 + s * 128: c * C0 + s * 128 + 128],
                                            idf[:])
                    tpd = k.inc("pe", last, 1)
                    k.wait("ve", "pe", tpd)
                    k.wait("ve", f"st{s % 2}")
                    rd = k.inc("ve", ve.scalar_tensor_tensor(
                        res[s % 2][:], py[:], c1[:, 0:1], xt[ti % 8][:],
                        ALU.mult, ALU.add, accum_out=macc[:, ti:ti + 1]), 1)
                    k.wait("se", "ve", rd)
                    k.inc("se", se.activation(dumpt[:], res[s % 2][:], AF.Square,
                                              accum_out=sqacc[:, ti:ti + 1]), 1)
                    k.wait("sy", "ve", rd)
                    k.inc(f"st{s % 2}", sy.dma_start(
                        out=out_hbm[ti * 128:(ti + 1) * 128, :], in_=res[s % 2][:]), 16)
                k.barrier()
            k.flush()

    def phase_l1ffn(app):
        es = ExitStack()

        def p(name, shape, dt=F32):
            return es.enter_context(nc.sbuf_tensor(f"c{app}_{name}", shape, dt))

        with es:
            w1s = p("w1s", [128, 8 * D], BF16)
            w2s = p("w2s", [128, 8 * 512], BF16)
            w3s = p("w3s", [128, 4 * 256], BF16)
            w4s = p("w4s", [128, 2 * 128], BF16)
            outb = p("outb", [128, 8 * T], F32)
            tfraw = p("tfraw", [128, T], BF16)
            obf = p("obf", [128, 8 * TCF], BF16)
            h1b = p("h1b", [128, 8 * TCF], BF16)
            h2b = p("h2b", [128, 4 * TCF], BF16)
            h3b = p("h3b", [128, 2 * TCF], BF16)
            cbb = p("cbb", [128, TCF], F32)
            scr4k = p("scr4k", [128, 1024], F32)
            pta = scr4k[:, 0:TCF]
            ptb = scr4k[:, TCF:2 * TCF]
            cb1s = p("cb1s", [128, 8], F32)
            cb2s = p("cb2s", [128, 4], F32)
            cb3s = p("cb3s", [128, 2], F32)
            abcol = p("abcol", [128, 2], F32)
            b4col = p("b4col", [128, 1], F32)
            gfcol = p("gfcol", [128, 8], F32)
            gfb = p("gfb", [128, 8], BF16)
            xt = [p(f"xt{j}", [128, D], F32) for j in range(2)]
            xlnf = scr4k
            res = [scr4k, p("res1", [128, D], F32)]
            psT = es.enter_context(nc.psum_tensor(f"c{app}_psT", [128, 1024], F32))
            psF = [es.enter_context(nc.psum_tensor(f"c{app}_psF{j}", [128, 512], F32))
                   for j in range(2)]
            psG = es.enter_context(nc.psum_tensor(f"c{app}_psG", [128, 8], F32))
            psC = es.enter_context(nc.psum_tensor(f"c{app}_psC", [128, 1024], F32))

            k.inc("dwt", ge.dma_start(
                out=w1s[:].rearrange("p (c o) -> p c o", c=8),
                in_=dr["cw1"].rearrange("(c p) o -> p c o", p=128)), 16)
            k.inc("dwt", ge.dma_start(
                out=w2s[:].rearrange("p (c o) -> p c o", c=8),
                in_=dr["cw2"].rearrange("(c p) o -> p c o", p=128)), 16)
            k.inc("dwt", ge.dma_start(
                out=w3s[:].rearrange("p (c o) -> p c o", c=4),
                in_=dr["cw3"].rearrange("(c p) o -> p c o", p=128)), 16)
            k.inc("dwt", ge.dma_start(
                out=w4s[:].rearrange("p (c o) -> p c o", c=2),
                in_=dr["w4rep"].rearrange("(c p) o -> p c o", p=128)), 16)
            k.inc("dli", sy.dma_start(out=cb1s[:], in_=dr["cb1"]), 16)
            k.inc("dli", sy.dma_start(out=cb2s[:], in_=dr["cb2"]), 16)
            k.inc("dli", sy.dma_start(out=cb3s[:], in_=dr["cb3"]), 16)

            colmath_stats()
            lnds = {}
            for i in range(NT):
                j = i % 2
                if i >= 2:
                    k.wait("sy", "se", lnds[i - 2])
                k.inc(f"dl{j}", sy.dma_start(
                    out=xt[j][:], in_=out_hbm[i * 128:(i + 1) * 128, :]), 16)
                k.wait("se", f"dl{j}")
                k.wait("se", "pe")
                lnd = k.inc("se", se.activation(
                    xlnf[:], xt[j][:], AF.Identity,
                    scale=rcol[:, i:i + 1], bias=nmr[:, i:i + 1]), 1)
                lnds[i] = lnd
                k.wait("pe", "se", lnd)
                k.wait("pe", "se")
                ist = None
                for c in range(8):
                    ist = pe.transpose(psT[:, c * 128:(c + 1) * 128],
                                       xlnf[:, c * 128:(c + 1) * 128], idf[:])
                tpd = k.inc("pe", ist, 1)
                k.wait("se", "pe", tpd)
                lastc = None
                for c in range(8):
                    lastc = se.activation(
                        outb[:, c * T + i * 128: c * T + i * 128 + 128],
                        psT[:, c * 128:(c + 1) * 128], AF.Identity)
                k.inc("se", lastc, 1)
            k.barrier()
            for c in range(8):
                ve.tensor_reduce(gfcol[:, c:c + 1], outb[:, c * T:(c + 1) * T],
                                 AX.X, ALU.add)
            ve.tensor_scalar(gfcol[:], gfcol[:], 1.0 / T, None, ALU.mult)
            k.inc("ve", ve.tensor_copy(gfb[:], gfcol[:]), 1)
            k.wait("pe", "ve")
            gw1 = h1b
            gw2 = h2b[:, 0:4 * 256]
            gw3 = h3b[:, 0:2]
            gb1 = p("gb1s", [128, 4], F32)
            gb2 = p("gb2s", [128, 2], F32)
            gb3 = p("gb3s", [1, 1], F32)
            b4r = p("b4r", [1, 1], F32)
            arr = p("arr", [1, 1], F32)
            g1b = p("g1b", [128, 4], BF16)
            g2b = p("g2b", [128, 2], BF16)
            gfv = p("gfv", [1, 1], F32)
            abrow = p("abrow", [1, 2], F32)
            k.inc("dwt", ge.dma_start(
                out=gw1.rearrange("p (c o) -> p c o", c=8),
                in_=dr["gw1"].rearrange("(c p) o -> p c o", p=128)), 16)
            k.inc("dwt", ge.dma_start(
                out=gw2.rearrange("p (c o) -> p c o", c=4),
                in_=dr["gw2"].rearrange("(c p) o -> p c o", p=128)), 16)
            k.inc("dwt", ge.dma_start(out=gw3, in_=dr["gw3"]), 16)
            k.inc("dli", sy.dma_start(out=gb1[:], in_=dr["gb1"]), 16)
            k.inc("dli", sy.dma_start(out=gb2[:], in_=dr["gb2"]), 16)
            k.inc("dli", sy.dma_start(out=gb3[:], in_=dr["gb3"]), 16)
            k.inc("dli", sy.dma_start(out=b4r[:], in_=dr["b4"]), 16)
            k.inc("dli", sy.dma_start(out=arr[:], in_=dr["arange_"]), 16)
            k.wait("pe", "dwt")
            k.wait("pe", "dli")
            for mc2 in range(4):
                last = None
                for kc in range(8):
                    last = pe.matmul(psG[:, mc2:mc2 + 1],
                                     gw1[:, kc * 512 + mc2 * 128: kc * 512 + mc2 * 128 + 128],
                                     gfb[:, kc:kc + 1],
                                     start=(kc == 0), stop=(kc == 7))
                k.inc("pe", last, 1)
            k.wait("se", "pe")
            lastg = None
            for mc2 in range(4):
                lastg = se.activation(g1b[:, mc2:mc2 + 1], psG[:, mc2:mc2 + 1],
                                      AF.Gelu, bias=gb1[:, mc2:mc2 + 1])
            k.inc("se", lastg, 1)
            k.wait("pe", "se")
            for mc2 in range(2):
                last = None
                for kc in range(4):
                    last = pe.matmul(psG[:, 4 + mc2:5 + mc2],
                                     gw2[:, kc * 256 + mc2 * 128: kc * 256 + mc2 * 128 + 128],
                                     g1b[:, kc:kc + 1],
                                     start=(kc == 0), stop=(kc == 3))
                k.inc("pe", last, 1)
            k.wait("se", "pe")
            lastg = None
            for mc2 in range(2):
                lastg = se.activation(g2b[:, mc2:mc2 + 1], psG[:, 4 + mc2:5 + mc2],
                                      AF.Gelu, bias=gb2[:, mc2:mc2 + 1])
            k.inc("se", lastg, 1)
            k.wait("pe", "se")
            last = None
            for kc in range(2):
                last = pe.matmul(psG[0:1, 6:7], gw3[:, kc:kc + 1], g2b[:, kc:kc + 1],
                                 start=(kc == 0), stop=(kc == 1))
            k.inc("pe", last, 1)
            k.wait("se", "pe")
            k.inc("se", se.activation(gfv[:], psG[0:1, 6:7], AF.Sigmoid,
                                      bias=gb3[0:1, 0:1]), 1)
            k.wait("ve", "se")
            ve.tensor_scalar(abrow[:, 0:1], gfv[:], 0.7, -0.5, ALU.mult, ALU.add)
            ve.tensor_tensor(abrow[:, 0:1], abrow[:, 0:1], arr[:], ALU.mult)
            ve.tensor_scalar(abrow[:, 0:1], abrow[:, 0:1], 2.0, 0.1, ALU.mult, ALU.add)
            k.inc("ve", ve.tensor_scalar(abrow[:, 1:2], arr[:], 0.6, None, ALU.mult), 1)
            k.wait("pe", "ve")
            k.inc("pe", pe.matmul(psG[:, 0:2], onesrow[:], abrow[:],
                                  start=True, stop=True), 1)
            k.inc("pe", pe.matmul(psG[:, 2:3], onesrow[:], b4r[:],
                                  start=True, stop=True), 1)
            k.wait("ve", "pe")
            ve.tensor_copy(abcol[:], psG[:, 0:2])
            k.inc("ve", ve.tensor_copy(b4col[:], psG[:, 2:3]), 1)
            if dbg and app == 0:
                k.wait("sy", "ve")
                k.inc("dst", sy.dma_start(out=dbg10[0:128, 0:2], in_=abcol[:]), 16)
                k.inc("dst", sy.dma_start(out=dbg10[0:128, 2:3], in_=b4col[:], allow_slow_non_contiguous=True), 16)
                k.inc("dst", sy.dma_start(out=dbg10[0:128, 3:11], in_=gfcol[:]), 16)
                k.wait("sy", "dst")
            k.barrier()

            sign = [1.0, 1.0, -1.0, -1.0, -1.0, -1.0, 1.0, 1.0]
            src = [4, 5, 6, 7, 0, 1, 2, 3]
            for cyc in range(3):
                for tc in range(NTCF):
                    t0 = tc * TCF
                    k.wait("ve", "se")
                    k.wait("ve", "pe")
                    lastc = None
                    for c in range(8):
                        lastc = ve.tensor_copy(obf[:, c * TCF:(c + 1) * TCF],
                                               outb[:, c * T + t0: c * T + t0 + TCF])
                    obd = k.inc("ve", lastc, 1)
                    k.wait("pe", "ve", obd)
                    evd = {}
                    k.wait("pe", "se")
                    for oc in range(8):
                        pf = psF[oc % 2]
                        if oc % 2 in evd:
                            k.wait("pe", "se", evd[oc % 2])
                        last = None
                        for kc in range(8):
                            last = pe.matmul(
                                pf[:, 0:TCF],
                                w1s[:, kc * D + oc * 128: kc * D + oc * 128 + 128],
                                obf[:, kc * TCF:(kc + 1) * TCF],
                                start=(kc == 0), stop=(kc == 7))
                        mmd = k.inc("pe", last, 1)
                        k.wait("se", "pe", mmd)
                        evd[oc % 2] = k.inc("se", se.activation(
                            h1b[:, oc * TCF:(oc + 1) * TCF], pf[:, 0:TCF],
                            AF.Gelu, bias=cb1s[:, oc:oc + 1]), 1)
                    k.wait("pe", "se")
                    evd = {}
                    for oc in range(4):
                        pf = psF[oc % 2]
                        if oc % 2 in evd:
                            k.wait("pe", "se", evd[oc % 2])
                        last = None
                        for kc in range(8):
                            last = pe.matmul(
                                pf[:, 0:TCF],
                                w2s[:, kc * 512 + oc * 128: kc * 512 + oc * 128 + 128],
                                h1b[:, kc * TCF:(kc + 1) * TCF],
                                start=(kc == 0), stop=(kc == 7))
                        mmd = k.inc("pe", last, 1)
                        k.wait("se", "pe", mmd)
                        evd[oc % 2] = k.inc("se", se.activation(
                            h2b[:, oc * TCF:(oc + 1) * TCF], pf[:, 0:TCF],
                            AF.Gelu, bias=cb2s[:, oc:oc + 1]), 1)
                    k.wait("pe", "se")
                    for oc in range(2):
                        pf = psF[oc % 2]
                        last = None
                        for kc in range(4):
                            last = pe.matmul(
                                pf[:, 0:TCF],
                                w3s[:, kc * 256 + oc * 128: kc * 256 + oc * 128 + 128],
                                h2b[:, kc * TCF:(kc + 1) * TCF],
                                start=(kc == 0), stop=(kc == 3))
                        mmd = k.inc("pe", last, 1)
                        k.wait("se", "pe", mmd)
                        evd[oc % 2] = k.inc("se", se.activation(
                            h3b[:, oc * TCF:(oc + 1) * TCF], pf[:, 0:TCF],
                            AF.Gelu, bias=cb3s[:, oc:oc + 1]), 1)
                    pf = psF[0]
                    k.wait("pe", "se")
                    last = None
                    for kc in range(2):
                        last = pe.matmul(pf[:, 0:TCF],
                                         w4s[:, kc * 128:(kc + 1) * 128],
                                         h3b[:, kc * TCF:(kc + 1) * TCF],
                                         start=(kc == 0), stop=(kc == 1))
                    mmd = k.inc("pe", last, 1)
                    k.wait("se", "pe", mmd)
                    k.inc("se", se.activation(tfraw[:, t0:t0 + TCF], pf[:, 0:TCF],
                                              AF.Identity), 1)
                k.barrier()
                for tc in range(NTCF):
                    t0 = tc * TCF
                    k.wait("se", "ve")
                    ssd = k.inc("se", se.activation(cbb[:], tfraw[:, t0:t0 + TCF],
                                                    AF.Sigmoid, bias=b4col[:, 0:1]), 1)
                    k.wait("ve", "se", ssd)
                    ve.tensor_scalar(cbb[:], cbb[:], abcol[:, 1:2], abcol[:, 0:1],
                                     ALU.mult, ALU.add)
                    lastp = None
                    for a in (0, 1, 2, 3):
                        b = src[a]
                        oa = outb[:, a * T + t0: a * T + t0 + TCF]
                        ob = outb[:, b * T + t0: b * T + t0 + TCF]
                        ve.tensor_tensor(pta, cbb[:], ob, ALU.mult)
                        ve.tensor_tensor(ptb, cbb[:], oa, ALU.mult)
                        sa = c1 if sign[a] > 0 else cn1
                        sb_ = c1 if sign[b] > 0 else cn1
                        ve.scalar_tensor_tensor(oa, pta, sa[:, 0:1], oa,
                                                ALU.mult, ALU.add)
                        lastp = ve.scalar_tensor_tensor(ob, ptb, sb_[:, 0:1], ob,
                                                        ALU.mult, ALU.add)
                    k.inc("ve", lastp, 1)
                if dbg and app == 0 and cyc == 0:
                    k.wait("sy", "ve")
                    k.wait("sy", "se")
                    k.inc("dst", sy.dma_start(out=dbg8, in_=outb[:]), 16)
                    k.inc("dwt", ge.dma_start(out=dbg9, in_=tfraw[:]), 16)
                    k.wait("sy", "dst")
                    k.wait("sy", "dwt")
                k.barrier()

            rds = {}
            for i in range(NT):
                j = i % 2
                if i >= 2:
                    k.wait("sy", "ve", rds[i - 2])
                k.inc(f"dl{j}", sy.dma_start(
                    out=xt[j][:], in_=out_hbm[i * 128:(i + 1) * 128, :]), 16)
                k.wait("pe", "ve")
                last = None
                for c in range(8):
                    last = pe.transpose(psC[:, c * 128:(c + 1) * 128],
                                        outb[:, c * T + i * 128: c * T + i * 128 + 128],
                                        idf[:])
                tpd = k.inc("pe", last, 1)
                k.wait("ve", "pe", tpd)
                k.wait("ve", f"dl{j}")
                if i >= 2:
                    k.wait("ve", f"st{j}")
                rd = k.inc("ve", ve.scalar_tensor_tensor(
                    res[j][:], psC[:], chalf[:, 0:1], xt[j][:], ALU.mult, ALU.add,
                    accum_out=macc[:, i:i + 1]), 1)
                rds[i] = rd
                k.wait("se", "ve", rd)
                k.inc("se", se.activation(dumpt[:], res[j][:], AF.Square,
                                          accum_out=sqacc[:, i:i + 1]), 1)
                k.wait("sy", "ve", rd)
                k.inc(f"st{j}", sy.dma_start(
                    out=out_hbm[i * 128:(i + 1) * 128, :], in_=res[j][:]), 16)
            k.barrier()
            k.flush()

    def phase_l1attn():
        es = ExitStack()

        def p(name, shape, dt=F32):
            return es.enter_context(nc.sbuf_tensor(f"m_{name}", shape, dt))

        with es:
            wxz = p("wxz", [128, 8 * 2 * DI], BF16)
            opw = p("opw", [128, 16 * D], BF16)
            xpt = p("xpt", [128, 16 * 33], BF16)
            dtws = p("dtws", [1, 2048], BF16)
            alg = p("alg", [128, 256], F32)
            acoef = alg
            cwv = p("cwv", [128, 64], F32)
            cbv = p("cbv", [128, 16], F32)
            dtbv = p("dtbv", [128, 16], F32)
            dpv = p("dpv", [128, 16], F32)
            xt = [p(f"xt{j}", [128, D], F32) for j in range(2)]
            xlb = p("xlb", [128, D], BF16)
            xTb = p("xTb", [128, 8 * TC1], BF16)
            xin = p("xin", [128, 16 * (TC1 + 3)], BF16)
            ubf = p("ubf", [128, 16 * TC1], BF16)
            xcs = p("xcs", [128, TC1], F32)
            bcb = p("bcb", [33, TC1], BF16)
            sel = p("sel", [33, 33 * 128], BF16)
            dtrb = p("dtrb", [1, TC1], BF16)
            brep = p("brep", [128, 16 * TC1], BF16)
            cint = p("cint", [128, 16 * TC1], BF16)
            dsc = p("dsc", [128, TC1], F32)
            rsc = p("rsc", [128, TC1], F32)
            dub = p("dub", [128, TC1], BF16)
            asc = [p(f"asc{j}", [128, TC1], F32) for j in range(2)]
            bsc = [p(f"bsc{j}", [128, TC1], BF16) for j in range(2)]
            hint = p("hint", [128, 16 * TC1], BF16)
            tmpb = p("tmpb", [128, 16 * TC1], BF16)
            hstate = p("hstate", [128, 256], BF16)
            ybuf = p("ybuf", [128, 16 * TC1], BF16)
            zsb = p("zsb", [128, 16 * TC1], BF16)
            mbf = ybuf
            res0 = p("res0", [128, D], F32)
            res = [res0, res0]
            psT = es.enter_context(nc.psum_tensor("m_psT", [128, 1024], BF16))
            psA = [es.enter_context(nc.psum_tensor(f"m_psA{j}", [128, 512], F32))
                   for j in range(2)]
            ps33 = es.enter_context(nc.psum_tensor("m_ps33", [33, TC1], F32))
            psO = es.enter_context(nc.psum_tensor("m_psO", [128, 1024], F32))

            k.inc("dwt", ge.dma_start(
                out=wxz[:].rearrange("p (c o) -> p c o", c=8),
                in_=dr["wxzt"].rearrange("(c p) o -> p c o", p=128)), 16)
            k.inc("dwt", ge.dma_start(
                out=opw[:].rearrange("p (c o) -> p c o", c=16),
                in_=dr["opwt"].rearrange("(c p) o -> p c o", p=128)), 16)
            k.inc("dwt", ge.dma_start(
                out=xpt[:].rearrange("p (c o) -> p c o", c=16),
                in_=dr["xpt"].rearrange("(c p) o -> p c o", p=128)), 16)
            k.inc("dwt", ge.dma_start(out=dtws[:], in_=dr["dtw"]), 16)
            k.inc("dli", sy.dma_start(out=alg[:], in_=dr["alog"]), 16)
            k.inc("dli", sy.dma_start(out=cwv[:], in_=dr["convw"]), 16)
            k.inc("dli", sy.dma_start(out=cbv[:], in_=dr["convb"]), 16)
            k.inc("dli", sy.dma_start(out=dtbv[:], in_=dr["dtb"]), 16)
            k.inc("dli", sy.dma_start(out=dpv[:], in_=dr["dpar"]), 16)
            k.inc("dwt", ge.dma_start(out=sel[:], in_=dr["selmat"]), 16)
            colmath_stats()
            k.wait("se", "dli")
            k.inc("se", se.activation(acoef[:], alg[:], AF.Exp), 1)
            k.wait("ve", "se")
            ve.tensor_scalar(acoef[:], acoef[:], -1.0, None, ALU.mult)
            ve.memset(hstate[:], 0.0)
            k.inc("ve", ve.memset(xin[:], 0.0), 1)
            k.barrier()

            l1s = int(os.environ.get("KL1S", "9"))
            for tc in range(NTC1):
                t0 = tc * TC1
                for s in range(2):
                    ti = tc * 2 + s
                    buf = xt[s]
                    k.inc(f"dl{s}", sy.dma_start(
                        out=buf[:], in_=out_hbm[ti * 128:(ti + 1) * 128, :]), 16)
                    k.wait("se", f"dl{s}")
                    k.wait("se", "pe")
                    lnd = k.inc("se", se.activation(
                        xlb[:], buf[:], AF.Identity,
                        scale=rcol[:, ti:ti + 1], bias=nmr[:, ti:ti + 1]), 1)
                    k.wait("pe", "se", lnd)
                    k.wait("pe", "ve")
                    ist = None
                    for c in range(8):
                        ist = pe.transpose(psT[:, c * 128:(c + 1) * 128],
                                           xlb[:, c * 128:(c + 1) * 128], idb[:])
                    tpd = k.inc("pe", ist, 1)
                    k.wait("ve", "pe", tpd)
                    lastc = None
                    for c in range(8):
                        lastc = ve.tensor_copy(
                            xTb[:, c * TC1 + s * 128: c * TC1 + s * 128 + 128],
                            psT[:, c * 128:(c + 1) * 128])
                    k.inc("ve", lastc, 1)
                if l1s <= 1:
                    k.barrier()
                    continue
                k.wait("pe", "ve")
                for dc in range(16):
                    pa = psA[dc % 2]
                    k.wait("pe", "ve")
                    last = None
                    for kc in range(8):
                        last = pe.matmul(
                            pa[:, 0:TC1],
                            wxz[:, kc * 2 * DI + dc * 128: kc * 2 * DI + dc * 128 + 128],
                            xTb[:, kc * TC1:(kc + 1) * TC1],
                            start=(kc == 0), stop=(kc == 7))
                    mmd = k.inc("pe", last, 1)
                    k.wait("ve", "pe", mmd)
                    xb = xin[:, dc * (TC1 + 3): (dc + 1) * (TC1 + 3)]
                    ve.tensor_copy(xb[:, 0:3], xb[:, TC1:TC1 + 3])
                    k.inc("ve", ve.tensor_copy(xb[:, 3:3 + TC1], pa[:, 0:TC1]), 1)
                    ve.tensor_scalar_mul(xcs[:], xb[:, 0:TC1], cwv[:, dc * 4: dc * 4 + 1])
                    lastcv = None
                    for tau in range(1, 4):
                        lastcv = ve.scalar_tensor_tensor(
                            xcs[:], xb[:, tau:tau + TC1],
                            cwv[:, dc * 4 + tau: dc * 4 + tau + 1],
                            xcs[:], ALU.mult, ALU.add)
                    cvd = k.inc("ve", lastcv, 1)
                    k.wait("se", "ve", cvd)
                    k.inc("se", se.activation(ubf[:, dc * TC1:(dc + 1) * TC1], xcs[:],
                                              AF.Silu, bias=cbv[:, dc:dc + 1]), 1)
                if l1s <= 2:
                    k.barrier()
                    continue
                for dc in range(16):
                    pa = psA[dc % 2]
                    k.wait("pe", "se")
                    k.wait("pe", "ve")
                    last = None
                    for kc in range(8):
                        last = pe.matmul(
                            pa[:, 0:TC1],
                            wxz[:, kc * 2 * DI + DI + dc * 128:
                                kc * 2 * DI + DI + dc * 128 + 128],
                            xTb[:, kc * TC1:(kc + 1) * TC1],
                            start=(kc == 0), stop=(kc == 7))
                    mmd = k.inc("pe", last, 1)
                    k.wait("se", "pe", mmd)
                    k.inc("se", se.activation(zsb[:, dc * TC1:(dc + 1) * TC1],
                                              pa[:, 0:TC1], AF.Silu), 1)
                if l1s <= 3:
                    k.barrier()
                    continue
                k.wait("pe", "se")
                last = None
                for dc in range(16):
                    last = pe.matmul(ps33[:],
                                     xpt[:, dc * 33:(dc + 1) * 33],
                                     ubf[:, dc * TC1:(dc + 1) * TC1],
                                     start=(dc == 0), stop=(dc == 15))
                mmd = k.inc("pe", last, 1)
                k.wait("se", "pe", mmd)
                se.activation(bcb[:], ps33[:], AF.Identity)
                ssd = k.inc("se", se.activation(dtrb[:], ps33[0:1, :], AF.Identity), 1)
                k.wait("pe", "se", ssd)
                for n in range(NST):
                    pa = psA[n % 2]
                    k.wait("pe", "se")
                    bd = k.inc("pe", pe.matmul(
                        pa[:, 0:TC1], sel[:, (1 + n) * 128:(2 + n) * 128],
                        bcb[:], start=True, stop=True), 1)
                    k.wait("se", "pe", bd)
                    k.inc("se", se.activation(brep[:, n * TC1:(n + 1) * TC1],
                                              pa[:, 0:TC1], AF.Identity), 1)
                for n in range(NST):
                    pa = psA[n % 2]
                    k.wait("pe", "se")
                    bd = k.inc("pe", pe.matmul(
                        pa[:, 0:TC1], sel[:, (17 + n) * 128:(18 + n) * 128],
                        bcb[:], start=True, stop=True), 1)
                    k.wait("se", "pe", bd)
                    ci3 = cint[:].rearrange("p (t n) -> p t n", n=NST)
                    k.inc("se", se.activation(ci3[:, :, n], pa[:, 0:TC1], AF.Identity), 1)
                if l1s <= 4:
                    k.barrier()
                    continue
                for dc in range(16):
                    pa = psA[dc % 2]
                    k.wait("pe", "se")
                    k.wait("pe", "ve")
                    dtd = k.inc("pe", pe.matmul(
                        pa[:, 0:TC1], dtws[0:1, dc * 128:(dc + 1) * 128],
                        dtrb[:], start=True, stop=True), 1)
                    k.wait("ve", "pe", dtd)
                    vrd = k.inc("ve", ve.tensor_scalar(
                        rsc[:], pa[:, 0:TC1], dtbv[:, dc:dc + 1], 0.0,
                        ALU.add, ALU.max), 1)
                    k.wait("se", "pe", dtd)
                    k.wait("se", "ve", vrd)
                    se.activation(dsc[:], pa[:, 0:TC1], AF.Abs, bias=dtbv[:, dc:dc + 1])
                    se.activation(dsc[:], dsc[:], AF.Exp, scale=cn1[:, 0:1])
                    sd = k.inc("se", se.activation(dsc[:], dsc[:], AF.Ln,
                                                   bias=c1[:, 0:1]), 1)
                    k.wait("ve", "se", sd)
                    dd = k.inc("ve", ve.tensor_tensor(dsc[:], dsc[:], rsc[:], ALU.add), 1)
                    dud = k.inc("ve", ve.tensor_tensor(
                        dub[:], dsc[:], ubf[:, dc * TC1:(dc + 1) * TC1], ALU.mult), 1)
                    k.wait("se", "ve", dd)
                    if l1s <= 5:
                        continue
                    hi3 = hint[:].rearrange("p (t n) -> p t n", n=NST)
                    a_used = {}
                    for n in range(NST):
                        jj = n % 2
                        if n >= 2:
                            k.wait("se", "ve", a_used[n - 2])
                        ad = k.inc("se", se.activation(
                            asc[jj][:], dsc[:], AF.Exp,
                            scale=acoef[:, dc * 16 + n: dc * 16 + n + 1]), 1)
                        ve.tensor_tensor(
                            bsc[jj][:], dub[:], brep[:, n * TC1:(n + 1) * TC1],
                            ALU.mult)
                        k.wait("ve", "se", ad)
                        a_used[n] = k.inc("ve", ve.tensor_tensor_scan(
                            hi3[:, :, n], asc[jj][:], bsc[jj][:],
                            hstate[:, dc * 16 + n: dc * 16 + n + 1],
                            ALU.mult, ALU.add), 1)
                    if l1s <= 6:
                        continue
                    ve.tensor_copy(hstate[:, dc * 16:(dc + 1) * 16],
                                   hint[:, (TC1 - 1) * 16: TC1 * 16])
                    ve.tensor_tensor(tmpb[:], hint[:], cint[:], ALU.mult)
                    tm3 = tmpb[:].rearrange("p (t n) -> p t n", n=NST)
                    k.lp_flag = True
                    ve.tensor_reduce(ybuf[:, dc * TC1:(dc + 1) * TC1], tm3,
                                     AX.X, ALU.add)
                    k.lp_flag = False
                    k.inc("ve", ve.scalar_tensor_tensor(
                        ybuf[:, dc * TC1:(dc + 1) * TC1],
                        ubf[:, dc * TC1:(dc + 1) * TC1], dpv[:, dc:dc + 1],
                        ybuf[:, dc * TC1:(dc + 1) * TC1], ALU.mult, ALU.add), 1)
                if l1s <= 7:
                    k.barrier()
                    continue
                md = k.inc("ve", ve.tensor_tensor(mbf[:], ybuf[:], zsb[:], ALU.mult), 1)
                k.wait("pe", "ve", md)
                for sl in range(2):
                    ti = tc * 2 + sl
                    k.wait("pe", "ve")
                    last = None
                    for h in range(2):
                        for dc in range(16):
                            last = pe.matmul(
                                psO[:, h * 512:(h + 1) * 512],
                                mbf[:, dc * TC1 + sl * 128: dc * TC1 + sl * 128 + 128],
                                opw[:, dc * D + h * 512: dc * D + h * 512 + 512],
                                start=(dc == 0), stop=(dc == 15))
                    mmd = k.inc("pe", last, 1)
                    k.wait("ve", "pe", mmd)
                    k.wait("ve", "st0")
                    k.wait("ve", "st1")
                    rd = k.inc("ve", ve.scalar_tensor_tensor(
                        res[sl][:], psO[:], c1[:, 0:1], xt[sl][:],
                        ALU.mult, ALU.add, accum_out=macc[:, ti:ti + 1]), 1)
                    k.wait("se", "ve", rd)
                    k.inc("se", se.activation(dumpt[:], res[sl][:], AF.Square,
                                              accum_out=sqacc[:, ti:ti + 1]), 1)
                    k.wait("sy", "ve", rd)
                    k.inc(f"st{sl}", sy.dma_start(
                        out=out_hbm[ti * 128:(ti + 1) * 128, :], in_=res[sl][:]), 16)
                k.barrier()
            k.flush()

    phases = [
        lambda: phase_l0ffn(0, x_in, first=True),
        lambda: phase_l0attn(),
        lambda: phase_l0ffn(1, out_hbm, first=False),
        lambda: phase_l1ffn(0),
        lambda: phase_l1attn(),
        lambda: phase_l1ffn(1),
    ]
    for ph in phases[:nphases]:
        ph()
    k.wait("sy", "dst")
    k.barrier()
    k.flush()

    bes.close()
    es0.close()
    return nc


def _prep_host(inputs):
    f32 = np.float32
    g = {}
    g["bwt"] = np.ascontiguousarray(np.asarray(inputs["L0_attn_B_weight"], f32).T)
    g["omega"] = np.ascontiguousarray(
        np.asarray(inputs["L0_attn_omega_raw"], f32).reshape(8, 128).T)
    g["wqt"] = np.ascontiguousarray(
        np.asarray(inputs["L0_ffn_weight_q"], np.int8).T
        .reshape(8, 128, D).transpose(1, 0, 2).reshape(128, 8 * D))
    ws = np.asarray(inputs["L0_ffn_weight_scale"], f32)
    g["wsrep"] = np.ascontiguousarray(np.broadcast_to(ws[None, :], (128, D)))
    g["biasrep"] = np.ascontiguousarray(np.broadcast_to(
        np.asarray(inputs["L0_ffn_bias"], f32)[None, :], (128, D)))
    g["wxzt"] = np.ascontiguousarray(np.asarray(inputs["L1_attn_in_proj_w"], f32).T)
    cw = np.asarray(inputs["L1_attn_conv_w"], f32)[:, 0, :]
    g["convw"] = np.ascontiguousarray(
        cw.reshape(16, 128, 4).transpose(1, 0, 2).reshape(128, 64))
    g["convb"] = np.ascontiguousarray(
        np.asarray(inputs["L1_attn_conv_b"], f32).reshape(16, 128).T)
    g["dtb"] = np.ascontiguousarray(
        np.asarray(inputs["L1_attn_dt_b"], f32).reshape(16, 128).T)
    g["dpar"] = np.ascontiguousarray(
        np.asarray(inputs["L1_attn_D_param"], f32).reshape(16, 128).T)
    g["xpt"] = np.ascontiguousarray(np.asarray(inputs["L1_attn_x_proj_w"], f32).T)
    g["dtw"] = np.ascontiguousarray(
        np.asarray(inputs["L1_attn_dt_w"], f32)[:, 0].reshape(1, 2048))
    g["alog"] = np.ascontiguousarray(
        np.asarray(inputs["L1_attn_A_log"], f32).reshape(16, 128, 16)
        .transpose(1, 0, 2).reshape(128, 256))
    g["opwt"] = np.ascontiguousarray(np.asarray(inputs["L1_attn_out_proj_w"], f32).T)
    g["cw1"] = np.ascontiguousarray(np.asarray(inputs["L1_ffn_cn_w1"], f32))
    g["cb1"] = np.ascontiguousarray(
        np.asarray(inputs["L1_ffn_cn_b1"], f32).reshape(8, 128).T)
    g["cw2"] = np.ascontiguousarray(np.asarray(inputs["L1_ffn_cn_w2"], f32))
    g["cb2"] = np.ascontiguousarray(
        np.asarray(inputs["L1_ffn_cn_b2"], f32).reshape(4, 128).T)
    g["cw3"] = np.ascontiguousarray(np.asarray(inputs["L1_ffn_cn_w3"], f32))
    g["cb3"] = np.ascontiguousarray(
        np.asarray(inputs["L1_ffn_cn_b3"], f32).reshape(2, 128).T)
    g["w4rep"] = np.ascontiguousarray(
        np.broadcast_to(np.asarray(inputs["L1_ffn_cn_w4"], f32), (256, 128)))
    g["b4"] = np.asarray(inputs["L1_ffn_cn_b4"], f32).reshape(1, 1)
    g["gw1"] = np.ascontiguousarray(np.asarray(inputs["L1_ffn_gc_w1"], f32))
    g["gb1"] = np.ascontiguousarray(
        np.asarray(inputs["L1_ffn_gc_b1"], f32).reshape(4, 128).T)
    g["gw2"] = np.ascontiguousarray(np.asarray(inputs["L1_ffn_gc_w2"], f32))
    g["gb2"] = np.ascontiguousarray(
        np.asarray(inputs["L1_ffn_gc_b2"], f32).reshape(2, 128).T)
    g["gw3"] = np.ascontiguousarray(
        np.asarray(inputs["L1_ffn_gc_w3"], f32).reshape(2, 128).T)
    g["gb3"] = np.asarray(inputs["L1_ffn_gc_b3"], f32).reshape(1, 1)
    g["arange_"] = np.asarray(inputs["L1_ffn_adaptive_range"], f32).reshape(1, 1)
    g["identbf"] = np.eye(128).astype(ml_dtypes.bfloat16)
    selm = np.zeros((33, 33, 128), np.float32)
    for r in range(33):
        selm[r, r, :] = 1.0
    g["selmat"] = np.ascontiguousarray(selm.reshape(33, 33 * 128))
    g["identf"] = np.eye(128).astype(f32)
    return g


def kernel(nphases=6, dbg=False, **inputs):
    shared = _prep_host(inputs)
    x = np.asarray(inputs["L0_attn_x_seq"], np.float32)
    in_maps = []
    for b in range(8):
        m = dict(shared)
        m["x"] = np.ascontiguousarray(x[b])
        in_maps.append(m)
    nc = build_nc(nphases, dbg)
    res = run_bass_kernel_spmd(nc, in_maps, core_ids=list(range(8)))
    return np.stack([np.asarray(r["out"], np.float32) for r in res.results], axis=0)


if __name__ == "__main__":
    nc = build_nc()
    n = sum(1 for _ in nc.all_instructions())
    print("built ok, instructions:", n)
